# version 21
# baseline (speedup 1.0000x reference)
import math
import numpy as np
import ml_dtypes

import concourse.bass as bass
import concourse.mybir as mybir
from concourse.bass_utils import run_bass_kernel_spmd

F32 = mybir.dt.float32
BF16 = mybir.dt.bfloat16
AF = mybir.ActivationFunctionType
ALU = mybir.AluOpType
AX = mybir.AxisListType

NCORES = 8
D = 2048
NH = 32
DH = 64
NSINK = 16
HID = 1024
NE = 8
NDC = D // 128
NHC = HID // 128
NEG = -1.0e30

BPP = 212_800


class Op:
    __slots__ = ("i", "eng", "emit", "deps", "dma", "skey", "needed", "val", "tag")

    def __init__(self, i, eng, emit, deps, dma, skey, tag):
        self.i, self.eng, self.emit, self.deps, self.dma, self.skey, self.tag = (
            i, eng, emit, list(deps), dma, skey, tag)
        self.needed = False
        self.val = None


class Plan:

    def __init__(self, one_shot=()):
        self.ops = []
        self.one_shot = set(one_shot)

    def add(self, eng, emit, deps=(), dma=False, skey=None, tag=""):
        deps = [d for d in deps if d is not None]
        op = Op(len(self.ops), eng, emit, deps, dma,
                skey if dma else eng, tag)
        assert not dma or skey is not None
        self.ops.append(op)
        return op

    def add_dep(self, op, dep):
        if dep is not None:
            op.deps.append(dep)

    def _needs_wait(self, consumer, dep):
        return True

    def finalize(self, nc):
        for op in self.ops:
            if op.dma:
                op.needed = True
            for d in op.deps:
                if self._needs_wait(op, d):
                    d.needed = True
        cnt = {}
        for op in self.ops:
            if op.needed:
                cnt[op.skey] = cnt.get(op.skey, 0) + (16 if op.dma else 1)
                op.val = cnt[op.skey]
        for op in self.ops:
            if op.needed and op.skey in self.one_shot:
                op.val = cnt[op.skey]
        self.final_counts = cnt
        self.sems = {k: nc.alloc_semaphore("s_" + k) for k in cnt}
        from collections import deque
        succ = {}
        indeg = [0] * len(self.ops)
        last_on = {}
        for op in self.ops:
            if op.eng in last_on:
                succ.setdefault(last_on[op.eng].i, []).append(op.i)
                indeg[op.i] += 1
            last_on[op.eng] = op
            for d in op.deps:
                succ.setdefault(d.i, []).append(op.i)
                indeg[op.i] += 1
        q = deque(i for i in range(len(self.ops)) if indeg[i] == 0)
        seen = 0
        while q:
            i = q.popleft()
            seen += 1
            for j in succ.get(i, ()):
                indeg[j] -= 1
                if indeg[j] == 0:
                    q.append(j)
        assert seen == len(self.ops), f"dependency cycle: {len(self.ops) - seen} ops"

    def emit_engine(self, eng_name, eng):
        watermark = {}
        for op in self.ops:
            if op.eng != eng_name:
                continue
            want = {}
            for d in op.deps:
                if not self._needs_wait(op, d):
                    continue
                if d.val is None:
                    continue
                if max(watermark.get(d.skey, 0), want.get(d.skey, 0)) >= d.val:
                    continue
                want[d.skey] = d.val
            for k, v in want.items():
                watermark[k] = v
                eng.wait_ge(self.sems[k], v)
            inst = op.emit(eng)
            if op.needed:
                inst.then_inc(self.sems[op.skey], 16 if op.dma else 1)


def build_program(TOK):
    TH = TOK + 128
    NQT = TOK // 128
    NTT = TOK // 128
    NVT = TH // 128

    nc = bass.Bass("TRN2", target_bir_lowering=False, debug=False)

    d_ufm = nc.dram_tensor("u_fm_d", [128, NDC * TH], F32, kind="ExternalInput").ap()
    d_utm = nc.dram_tensor("u_tm", [TOK, D], F32, kind="ExternalInput").ap()
    d_wq = nc.dram_tensor("wq_d", [128, NDC * NDC * 128], F32, kind="ExternalInput").ap()
    d_wk = nc.dram_tensor("wk_d", [128, NDC * NDC * 128], F32, kind="ExternalInput").ap()
    d_wv = nc.dram_tensor("wv_d", [128, 4 * NDC * 512], F32, kind="ExternalInput").ap()
    d_bq = nc.dram_tensor("bq", [128, NDC], F32, kind="ExternalInput").ap()
    d_bk = nc.dram_tensor("bk", [128, NDC], F32, kind="ExternalInput").ap()
    d_bvrow = nc.dram_tensor("bv_row", [1, D], F32, kind="ExternalInput").ap()
    d_wo = nc.dram_tensor("wo_d", [128, 4 * NDC * 512], F32, kind="ExternalInput").ap()
    d_borow = nc.dram_tensor("bo_row", [1, D], F32, kind="ExternalInput").ap()
    d_sinksN = nc.dram_tensor("sinksN", [NSINK, NH * DH], F32, kind="ExternalInput").ap()
    d_sinksT = nc.dram_tensor("sinksT", [128, NH * NSINK], F32, kind="ExternalInput").ap()
    d_Cq = nc.dram_tensor("Cq", [128, TOK], F32, kind="ExternalInput").ap()
    d_Sq = nc.dram_tensor("Sq", [128, TOK], F32, kind="ExternalInput").ap()
    d_Ck = nc.dram_tensor("Ck", [128, TH], F32, kind="ExternalInput").ap()
    d_Sk = nc.dram_tensor("Sk", [128, TH], F32, kind="ExternalInput").ap()
    d_masks = nc.dram_tensor("masks", [128, 3 * 272], F32, kind="ExternalInput").ap()
    d_ident = nc.dram_tensor("ident", [128, 128], F32, kind="ExternalInput").ap()
    d_perm = nc.dram_tensor("perm", [128, 128], F32, kind="ExternalInput").ap()
    d_gate = nc.dram_tensor("gate_d", [128, NDC * NE], F32, kind="ExternalInput").ap()
    d_w13 = nc.dram_tensor("w13_d", [NE + 1, 128, 4 * NDC * 512], F32,
                           kind="ExternalInput").ap()
    d_w2 = nc.dram_tensor("w2_d", [NE + 1, 128, 4 * NHC * 512], F32, kind="ExternalInput").ap()
    d_constv = nc.dram_tensor("constv", [128, 4], F32, kind="ExternalInput").ap()
    d_out = nc.dram_tensor("y_out", [TOK, D], F32, kind="ExternalOutput").ap()

    BASE = (int(nc._init_sbuf_base) + 31) // 32 * 32

    def sb(name, off, shape, dtype):
        return nc.alloc_sbuf_tensor_at(name, shape, dtype, offset=BASE + off)

    o = 0
    masks = sb("masks", o, [128, 3 * 272], F32); o += 3264
    ident = sb("ident", o, [128, 128], BF16); o += 256
    perm = sb("perm", o, [128, 128], BF16); o += 256
    sinksT = sb("sinksT", o, [128, NH * NSINK], BF16); o += 1024
    sinksN = sb("sinksN", o, [NSINK, NH * DH], BF16); o += 4096
    bq = sb("bq", o, [128, NDC], F32); o += 64
    bk = sb("bk", o, [128, NDC], F32); o += 64
    borow = sb("borow", o, [1, D], BF16); o += 4096
    bvrow = sb("bvrow", o, [1, D], BF16); o += 4096
    ones = sb("ones", o, [1, 128], BF16); o += 256
    gateT = sb("gateT", o, [128, NDC * NE], BF16); o += 256
    m_sb = sb("m_sb", o, [128, NTT * 9], F32); o += 4 * NTT * 9
    o = (o + 31) // 32 * 32
    negmax = sb("negmax", o, [128, 4], F32); o += 32
    rowsum = sb("rowsum", o, [128, 4], F32); o += 32
    recipv = sb("recipv", o, [128, 4], F32); o += 32
    ln_s = sb("ln_s", o, [128, 8], F32); o += 32
    ln_ss = sb("ln_ss", o, [128, 8], F32); o += 32
    ln_mean = sb("ln_mean", o, [128, 8], F32); o += 32
    ln_msq = sb("ln_msq", o, [128, 8], F32); o += 32
    ln_var = sb("ln_var", o, [128, 8], F32); o += 32
    ln_rstd = sb("ln_rstd", o, [128, 8], F32); o += 32
    ln_nmrs = sb("ln_nmrs", o, [128, 8], F32); o += 32
    sc_gate = sb("sc_gate", o, [128, 64], F32); o += 256
    g_max1 = sb("g_max1", o, [128, 8], F32); o += 32
    g_eq = sb("g_eq", o, [128, 64], F32); o += 256
    g_msk = sb("g_msk", o, [128, 64], F32); o += 256
    g_max2 = sb("g_max2", o, [128, 8], F32); o += 32
    g_ge = sb("g_ge", o, [128, 64], F32); o += 256
    constv = sb("constv", o, [128, 4], F32); o += 32
    nc.const_aps.register(F32, 1.0 / D, constv[:, 0:1])
    nc.const_aps.register(F32, 1e-5, constv[:, 1:2])
    nc.const_aps.register(F32, -1.0, constv[:, 2:3])
    nc.const_aps.register(F32, NEG, constv[:, 3:4])
    FIXED_END = (o + 31) // 32 * 32

    A0 = FIXED_END
    u_fm = sb("u_fm", A0, [128, NDC * TH], BF16)
    y_fm = sb("y_fm", A0, [128, NDC * TOK], BF16)
    x1bf0 = sb("x1bf0", A0 + NDC * TOK * 2, [128, D], BF16)
    h_tm = sb("h_tm", A0, [128, NTT * HID], BF16)
    h_fm = sb("h_fm", A0 + NTT * HID * 2, [128, NHC * TOK], BF16)
    szA = NDC * TH * 2

    B0 = A0 + szA
    CS_OFF = B0
    Cq_sb = sb("Cq", CS_OFF, [128, TOK], F32)
    Sq_sb = sb("Sq", CS_OFF + TOK * 4, [128, TOK], F32)
    Ck_sb = sb("Ck", CS_OFF + TOK * 8, [128, TH], F32)
    Sk_sb = sb("Sk", CS_OFF + TOK * 8 + TH * 4, [128, TH], F32)
    WQ_OFF = CS_OFF + TOK * 8 + TH * 8
    wq_ring = [sb(f"wqr{i}", WQ_OFF + i * NDC * 128 * 2, [128, NDC * 128], BF16)
               for i in range(2)]
    RP_OFF = WQ_OFF + 2 * NDC * 128 * 2
    qraw = [sb(f"qraw{i}", RP_OFF + i * 1024, [128, 512], BF16) for i in range(2)]
    t1r = [sb(f"t1r{i}", RP_OFF + 2048 + i * 1024, [128, 512], BF16) for i in range(2)]
    t2r = [sb(f"t2r{i}", RP_OFF + 4096 + i * 1024, [128, 512], BF16) for i in range(2)]
    QF_OFF = RP_OFF + 6144
    q_fm = sb("q_fm", QF_OFF, [128, NDC * TOK], BF16)
    wv_ring = [sb(f"wvr{i}", QF_OFF + i * NDC * 512 * 2, [128, NDC * 512], BF16)
               for i in range(2)]
    szB = QF_OFF + NDC * TOK * 2 - B0
    acc = sb("acc", B0, [128, NTT * D], F32)
    assert NTT * D * 4 <= szB + 2048, (NTT * D * 4, szB)
    szB = max(szB, NTT * D * 4)

    C0 = B0 + szB
    k_fm = sb("k_fm", C0, [128, NDC * TH], BF16)
    x1_fm = sb("x1_fm", C0, [128, NDC * TOK], BF16)
    sqs = sb("sqs", C0 + NDC * TOK * 2, [128, D], BF16)
    szC = NDC * TH * 2

    D0 = C0 + szC
    v_tm = sb("v_tm", D0, [128, NVT * D], BF16)
    wo_ring = [sb(f"wor{i}", D0 + i * NDC * 512 * 2, [128, NDC * 512], BF16)
               for i in range(2)]
    w13_ring = [sb(f"w13r{i}", D0 + i * NDC * 512 * 2, [128, NDC * 512], BF16)
                for i in range(2)]
    x1bf1 = sb("x1bf1", D0 + 2 * NDC * 512 * 2, [128, D], BF16)
    x1bf = [x1bf0, x1bf1]
    szD = NVT * D * 2

    E0 = D0 + szD
    S_sb = [sb(f"Ssb{i}", E0 + i * 1088, [128, 272], F32) for i in range(4)]
    p_sb = [sb(f"psb{i}", E0 + 4352 + i * 544, [128, 272], BF16) for i in range(4)]
    pn_sb = [sb(f"pnsb{i}", E0 + 6528 + i * 544, [128, 272], BF16) for i in range(4)]
    pT_sb = [sb(f"pTsb{i}", E0 + 8704 + i * 768, [128, 384], BF16) for i in range(4)]
    utm_ring = [sb(f"utmr{i}", E0 + i * 2048, [128, 512], F32) for i in range(8)]
    w2_ring = [sb(f"w2r{i}", E0 + i * NHC * 512 * 2, [128, NHC * 512], BF16)
               for i in range(2)]
    out_ring = [sb(f"outr{i}", E0 + i * D * 4, [128, D], F32) for i in range(2)]
    szE = 16384
    assert BASE + E0 + szE <= 229376, (BASE, E0 + szE)

    pf = [nc.alloc_psum_tensor(f"pf{i}", [128, 512], F32) for i in range(6)]
    pb = [nc.alloc_psum_tensor(f"pb{i}", [128, 1024], BF16) for i in range(2)]

    P = Plan(one_shot=("c0", "c1"))

    def gload(dst_ap, src_ap, skey, deps=(), tag="gload"):
        return P.add("gpsimd", lambda g, d=dst_ap, s=src_ap: g.dma_start(out=d, in_=s),
                     deps=deps, dma=True, skey=skey, tag=tag)

    def sload(dst_ap, src_ap, skey, deps=(), tag="sload"):
        return P.add("sync", lambda g, d=dst_ap, s=src_ap: g.dma_start(out=d, in_=s),
                     deps=deps, dma=True, skey=skey, tag=tag)

    ld_consts = []
    ld_consts.append(gload(ident[:], d_ident[:], "c0"))
    ld_consts.append(gload(perm[:], d_perm[:], "c0"))
    ld_consts.append(gload(sinksT[:], d_sinksT[:], "c0"))
    ld_consts.append(gload(sinksN[:], d_sinksN[:], "c0"))
    ld_consts.append(gload(borow[:], d_borow[:], "c0"))
    ld_consts.append(gload(bvrow[:], d_bvrow[:], "c0"))
    ld_consts.append(gload(gateT[:], d_gate[:], "c0"))
    op_ones = P.add("gpsimd", lambda g: g.memset(ones[:], 1.0), tag="ones")
    op_mset = P.add("gpsimd", lambda g: g.memset(m_sb[:], 1.0), tag="m1")

    ld_ufm = []
    for i in range(4):
        src = d_ufm[:, i * 4 * TH:(i + 1) * 4 * TH]
        ld_ufm.append(gload(u_fm[:, i * 4 * TH:(i + 1) * 4 * TH], src, "c0",
                            tag="u_fm"))

    ld_masks = sload(masks[:], d_masks[:], "c1")
    ld_bq = sload(bq[:], d_bq[:], "c1")
    ld_bk = sload(bk[:], d_bk[:], "c1")
    ld_cs = [sload(Cq_sb[:], d_Cq[:], "c1"), sload(Sq_sb[:], d_Sq[:], "c1"),
             sload(Ck_sb[:], d_Ck[:], "c1"), sload(Sk_sb[:], d_Sk[:], "c1")]
    ld_constv = sload(constv[:], d_constv[:], "c1")

    v_evac = {}
    vb_prev = {}
    wv_ld = []
    for dvq in range(4):
        src = d_wv[:, dvq * NDC * 512:(dvq + 1) * NDC * 512]
        wv_ld.append(gload(wv_ring[dvq % 2][:], src, f"wv{dvq % 2}",
                           tag=f"wv{dvq}"))
    vmm = {}
    for dvq in range(4):
        for tt in range(NVT):
            bank = pf[(dvq * NVT + tt) % 2]
            def mk_vmm(dvq=dvq, tt=tt, bank=bank):
                def f(t):
                    last = None
                    for dc in range(NDC):
                        t.ldweights(u_fm[:, dc * TH + tt * 128: dc * TH + tt * 128 + 128])
                        last = t.matmul(bank[:, 0:512],
                                        u_fm[:, dc * TH + tt * 128: dc * TH + tt * 128 + 128],
                                        wv_ring[dvq % 2][:, dc * 512:(dc + 1) * 512],
                                        start=(dc == 0), stop=False)
                    t.ldweights(ones[0:1, :])
                    last = t.matmul(bank[:, 0:512], ones[0:1, :],
                                    bvrow[0:1, 2 * 512 * 0 + dvq * 512:(dvq + 1) * 512],
                                    start=False, stop=True)
                    return last
                return f
            deps = [wv_ld[dvq], ld_ufm[-1], op_ones,
                    ld_consts[5] if (dvq == 0 and tt == 0) else None,
                    v_evac.get((dvq * NVT + tt) - 2)]
            vmm[(dvq, tt)] = P.add("tensor", mk_vmm(), deps=deps, tag=f"vmm{dvq},{tt}")
            ev = P.add("scalar",
                       lambda s, dvq=dvq, tt=tt, bank=bank: s.copy(
                           v_tm[:, tt * D + dvq * 512: tt * D + (dvq + 1) * 512],
                           bank[:, 0:512]),
                       deps=[vmm[(dvq, tt)]], tag=f"vev{dvq},{tt}")
            v_evac[dvq * NVT + tt] = ev
        vb_prev[dvq] = vmm[(dvq, NVT - 1)]
        if dvq + 2 < 4:
            P.add_dep(wv_ld[dvq + 2], vb_prev[dvq])
    last_vev = v_evac[4 * NVT - 1]

    qk_mm = {}
    qk_ev = {}
    qk_perm = {}
    qk_t1 = {}
    qk_t2 = {}
    qk_add = {}
    wqk_ld = {}
    nreg = {}
    ridx = 0
    for og in range(32):
        isq = og < 16
        oc = og if isq else og - 16
        src = (d_wq if isq else d_wk)[:, oc * NDC * 128:(oc + 1) * NDC * 128]
        wqk_ld[og] = gload(wq_ring[og % 2][:], src, f"wq{og % 2}",
                           deps=[qk_mm.get(og - 2), wv_ld[-1] if og < 2 else None],
                           tag=f"wqk{og}")
        tss = [(64, 512), (576, 512)] if isq else [(0, 512), (512, 512), (1024, 128)]
        def mk_qkmm(og=og, tss=tss):
            def f(t):
                last = None
                for dc in range(NDC):
                    w = wq_ring[og % 2][:, dc * 128:(dc + 1) * 128]
                    t.ldweights(w)
                    for j, (t0, tn) in enumerate(tss):
                        last = t.matmul(pf[j][:, 0:tn], w,
                                        u_fm[:, dc * TH + t0: dc * TH + t0 + tn],
                                        start=(dc == 0), stop=(dc == NDC - 1))
                return last
            return f
        deps = [wqk_ld[og], ld_ufm[-1], last_vev if og == 0 else None]
        if og >= 1:
            deps += [qk_ev[(og - 1, j)] for j in range(nreg[og - 1])]
        nreg[og] = len(tss)
        qk_mm[og] = P.add("tensor", mk_qkmm(), deps=deps, tag=f"qkmm{og}")

        for j, (t0, tn) in enumerate(tss):
            r = ridx % 2
            bvec = bq if isq else bk
            def mk_ev(og=og, j=j, tn=tn, r=r, oc=oc, bvec=bvec):
                def f(s):
                    return s.activation(qraw[r][:, 0:tn],
                                        pf[j][:, 0:tn], AF.Identity,
                                        bias=bvec[:, oc:oc + 1], scale=1.0)
                return f
            qk_ev[(og, j)] = P.add("scalar", mk_ev(),
                                   deps=[qk_mm[og], ld_bq, ld_bk,
                                         qk_perm.get(ridx - 2), qk_t1.get(ridx - 2)],
                                   tag=f"qkev{og},{j}")
            def mk_perm(r=r, tn=tn, rb=4 + ridx % 2):
                def f(t):
                    t.ldweights(perm[:])
                    return t.matmul(pf[rb][:, 0:tn], perm[:], qraw[r][:, 0:tn],
                                    start=True, stop=True)
                return f
            qk_perm[ridx] = P.add("tensor", mk_perm(),
                                  deps=[qk_ev[(og, j)], ld_consts[1],
                                        qk_t2.get(ridx - 2)], tag=f"perm{ridx}")
            Ct = Cq_sb if isq else Ck_sb
            St = Sq_sb if isq else Sk_sb
            def mk_t1(r=r, t0=t0, tn=tn, Ct=Ct, isq=isq):
                c0 = (t0 - 64) if isq else t0
                def f(v):
                    return v.tensor_tensor(t1r[r][:, 0:tn], qraw[r][:, 0:tn],
                                           Ct[:, c0:c0 + tn], op=ALU.mult)
                return f
            qk_t1[ridx] = P.add("vector", mk_t1(),
                                deps=[qk_ev[(og, j)], ld_cs[0], ld_cs[2],
                                      qk_add.get(ridx - 2)], tag=f"t1_{ridx}")
            def mk_t2(r=r, t0=t0, tn=tn, St=St, isq=isq, rb=4 + ridx % 2):
                c0 = (t0 - 64) if isq else t0
                def f(v):
                    return v.tensor_tensor(t2r[r][:, 0:tn], pf[rb][:, 0:tn],
                                           St[:, c0:c0 + tn], op=ALU.mult)
                return f
            qk_t2[ridx] = P.add("vector", mk_t2(),
                                deps=[qk_perm[ridx], ld_cs[1], ld_cs[3],
                                      qk_add.get(ridx - 2)], tag=f"t2_{ridx}")
            dst = q_fm if isq else k_fm
            dcol = (oc * TOK + (t0 - 64)) if isq else (oc * TH + t0)
            def mk_add(r=r, tn=tn, dst=dst, dcol=dcol):
                def f(v):
                    return v.tensor_tensor(dst[:, dcol:dcol + tn], t1r[r][:, 0:tn],
                                           t2r[r][:, 0:tn], op=ALU.add)
                return f
            qk_add[ridx] = P.add("vector", mk_add(),
                                 deps=[qk_t1[ridx], qk_t2[ridx]], tag=f"radd{ridx}")
            ridx += 1
    n_regions = ridx
    last_q_add = qk_add[n_regions - 1]

    NIT = NH * NQT
    a_S = {}
    a_mask = {}
    a_max = {}
    a_exp = {}
    a_recip = {}
    a_pn = {}
    a_T = {}
    a_pTev = {}
    a_O = {}
    a_Oev = {}

    def it_hq(i):
        p, par = i // 2, i % 2
        return 2 * (p // NQT) + par, p % NQT

    def emit_S(i):
        h, qt = it_hq(i)
        ch, hp = h // 2, (h % 2) * 64
        bank = pf[i % 4]
        def f(t):
            qs = q_fm[hp:hp + 64, ch * TOK + qt * 128: ch * TOK + qt * 128 + 128]
            t.ldweights(qs, tile_position=(hp, 0))
            t.matmul(bank[0:128, 0:16], qs,
                     sinksT[hp:hp + 64, h * NSINK:(h + 1) * NSINK],
                     start=True, stop=False, tile_position=(hp, 0))
            return t.matmul(bank[0:128, 16:272], qs,
                            k_fm[hp:hp + 64, ch * TH + qt * 128: ch * TH + qt * 128 + 256],
                            start=False, stop=True, tile_position=(hp, 0))
        return f

    def emit_mask(i):
        h, qt = it_hq(i)
        typ = 0 if qt == 0 else (2 if qt == NQT - 1 else 1)
        def f(v):
            return v.tensor_tensor(S_sb[i % 4][:], pf[i % 4][0:128, 0:272],
                                   masks[:, typ * 272:(typ + 1) * 272], op=ALU.add)
        return f

    def emit_exp(i):
        def f(s):
            return s.activation(p_sb[i % 4][:], S_sb[i % 4][:], AF.Exp,
                                bias=0.0, scale=1.0,
                                accum_out=rowsum[:, i % 4:i % 4 + 1])
        return f

    def emit_recip(i):
        def f(v):
            return v.reciprocal(recipv[:, i % 4:i % 4 + 1], rowsum[:, i % 4:i % 4 + 1])
        return f

    def emit_pn(i):
        def f(s):
            return s.activation(pn_sb[i % 4][:], p_sb[i % 4][:], AF.Identity,
                                bias=0.0, scale=recipv[:, i % 4:i % 4 + 1])
        return f

    def emit_T(i):
        c0 = ((i // 2) % 2) * 512
        def f(t):
            t.transpose(pb[i % 2][0:128, c0:c0 + 128], pn_sb[i % 4][:, 16:144], ident[:])
            t.transpose(pb[i % 2][0:128, c0 + 128:c0 + 256], pn_sb[i % 4][:, 144:272], ident[:])
            return t.transpose(pb[i % 2][0:16, c0 + 256:c0 + 384], pn_sb[i % 4][:, 0:16], ident[:])
        return f

    def emit_pTev(i):
        c0 = ((i // 2) % 2) * 512
        def f(v):
            v.tensor_copy(pT_sb[i % 4][:, 0:256], pb[i % 2][0:128, c0:c0 + 256])
            return v.tensor_copy(pT_sb[i % 4][0:16, 256:384], pb[i % 2][0:16, c0 + 256:c0 + 384])
        return f

    def emit_O(i):
        h, qt = it_hq(i)
        bank = pf[4 + i % 2]
        def f(t):
            v0 = v_tm[:, qt * D + h * DH: qt * D + (h + 1) * DH]
            t.ldweights(v0)
            t.matmul(bank[0:64, 0:128], v0, pT_sb[i % 4][:, 0:128],
                     start=True, stop=False)
            v1 = v_tm[:, (qt + 1) * D + h * DH: (qt + 1) * D + (h + 1) * DH]
            t.ldweights(v1)
            t.matmul(bank[0:64, 0:128], v1, pT_sb[i % 4][:, 128:256],
                     start=False, stop=False)
            sn = sinksN[0:NSINK, h * DH:(h + 1) * DH]
            t.ldweights(sn)
            return t.matmul(bank[0:64, 0:128], sn, pT_sb[i % 4][0:16, 256:384],
                            start=False, stop=True)
        return f

    def emit_Oev(i):
        h, qt = it_hq(i)
        ch, hp = h // 2, (h % 2) * 64
        def f(s):
            return s.copy(y_fm[hp:hp + 64, ch * TOK + qt * 128: ch * TOK + qt * 128 + 128],
                          pf[4 + i % 2][0:64, 0:128])
        return f

    for w in range(0, NIT + 8, 2):
        for k in (w - 6, w - 5):
            if not (0 <= k < NIT):
                continue
            a_O[k] = P.add("tensor", emit_O(k),
                           deps=[a_pTev[k], a_Oev.get(k - 2),
                                 ld_consts[3] if k == 0 else None,
                                 last_vev if k == 0 else None], tag=f"O{k}")
            a_Oev[k] = P.add("scalar", emit_Oev(k), deps=[a_O[k]], tag=f"Oe{k}")
        for j in (w - 4, w - 3):
            if not (0 <= j < NIT):
                continue
            a_T[j] = P.add("tensor", emit_T(j),
                           deps=[a_pn[j], a_pTev.get(j - 2), ld_consts[0]], tag=f"T{j}")
            a_pTev[j] = P.add("vector", emit_pTev(j),
                              deps=[a_T[j], a_O.get(j - 4)], tag=f"pTe{j}")
        for i in (w, w + 1):
            if i >= NIT:
                continue
            a_S[i] = P.add("tensor", emit_S(i),
                           deps=[a_mask.get(i - 4),
                                 last_q_add if i == 0 else None,
                                 ld_consts[2] if i == 0 else None],
                           tag=f"S{i}")
            a_mask[i] = P.add("vector", emit_mask(i),
                              deps=[a_S[i], a_exp.get(i - 4), ld_masks], tag=f"mk{i}")
            a_exp[i] = P.add("scalar", emit_exp(i),
                             deps=[a_mask[i], a_pn.get(i - 4), a_recip.get(i - 4)],
                             tag=f"ex{i}")
            a_recip[i] = P.add("vector", emit_recip(i),
                               deps=[a_exp[i], a_pn.get(i - 4)], tag=f"rc{i}")
            a_pn[i] = P.add("scalar", emit_pn(i),
                            deps=[a_recip[i], a_T.get(i - 4)], tag=f"pn{i}")
    last_att_te = a_O[NIT - 1]
    last_att_ev = a_Oev[NIT - 1]

    wo_ld = {}
    wo_mm = {}
    ev_add = {}
    utm_ld = {}
    def wo_load(dq, deps):
        src = d_wo[:, dq * NDC * 512:(dq + 1) * NDC * 512]
        wo_ld[dq] = gload(wo_ring[dq % 2][:], src, f"wo{dq % 2}",
                          deps=deps, tag=f"woL{dq}")

    def utm_issue(dq):
        for tt in range(NTT):
            li = dq * NTT + tt
            utm_ld[(dq, tt)] = sload(
                utm_ring[li % 8][:],
                d_utm[tt * 128:(tt + 1) * 128, dq * 512:(dq + 1) * 512],
                f"utm{li % 8}",
                deps=[ev_add.get(li - 8) if li >= 8 else last_att_te],
                tag=f"utm{dq},{tt}")

    def wo_tile(dq, tt):
        li = dq * NTT + tt
        bank = pf[li % 2]
        def mk_womm(dq=dq, tt=tt, bank=bank):
            def f(t):
                for yc in range(NDC):
                    ys = y_fm[:, yc * TOK + tt * 128: yc * TOK + tt * 128 + 128]
                    t.ldweights(ys)
                    t.matmul(bank[:, 0:512], ys,
                             wo_ring[dq % 2][:, yc * 512:(yc + 1) * 512],
                             start=(yc == 0), stop=False)
                t.ldweights(ones[0:1, :])
                return t.matmul(bank[:, 0:512], ones[0:1, :],
                                borow[0:1, dq * 512:(dq + 1) * 512],
                                start=False, stop=True)
            return f
        wo_mm[(dq, tt)] = P.add("tensor", mk_womm(),
                                deps=[wo_ld[dq], last_att_ev if li == 0 else None,
                                      ld_consts[4], op_ones,
                                      ev_add.get(li - 2)], tag=f"wo{dq},{tt}")
        def mk_evadd(dq=dq, tt=tt, bank=bank, li=li):
            def f(v):
                return v.tensor_tensor(acc[:, tt * D + dq * 512: tt * D + (dq + 1) * 512],
                                       bank[:, 0:512], utm_ring[li % 8][:],
                                       op=ALU.add)
            return f
        ev_add[li] = P.add("vector", mk_evadd(),
                           deps=[wo_mm[(dq, tt)], utm_ld[(dq, tt)],
                                 last_att_te if li == 0 else None],
                           tag=f"eva{dq},{tt}")

    ln_done = {}
    x1eva = {}
    x1evb = {}
    x1Tb_d = {}
    gate_m = {}
    gmm = {}
    g1p = [None]

    def ln_chain(tt, dep0, accs, pre):
        cs = slice(tt, tt + 1)
        s1 = P.add("vector", lambda v, a=accs, cs=cs: v.reduce_sum(ln_s[:, cs], a, axis=AX.X),
                   deps=dep0, tag=f"{pre}s{tt}")
        s2 = P.add("scalar", lambda s, a=accs, cs=cs: s.activation(
                        sqs[:], a, AF.Square, accum_out=ln_ss[:, cs]),
                   deps=dep0 + [ln_chain.prev_s3], tag=f"{pre}sq{tt}")
        ln_chain.prev_s3 = s2
        s4 = P.add("scalar", lambda s, cs=cs: s.mul(ln_mean[:, cs], ln_s[:, cs], 1.0 / D),
                   deps=[s1, ld_constv], tag=f"{pre}mean{tt}")
        s5 = P.add("scalar", lambda s, cs=cs: s.mul(ln_msq[:, cs], ln_ss[:, cs], 1.0 / D),
                   deps=[s2], tag=f"{pre}msq{tt}")
        s7 = P.add("vector", lambda v, cs=cs: v.scalar_tensor_tensor(
                        ln_var[:, cs], ln_mean[:, cs], ln_mean[:, cs], ln_msq[:, cs],
                        op0=ALU.mult, op1=ALU.subtract),
                   deps=[s4, s5], tag=f"{pre}var{tt}")
        s8 = P.add("scalar", lambda s, cs=cs: s.activation(
                        ln_var[:, cs], ln_var[:, cs], AF.Identity,
                        bias=constv[:, 1:2], scale=-1.0),
                   deps=[s7], tag=f"{pre}vpe{tt}")
        s9 = P.add("vector", lambda v, cs=cs: v.reciprocal(ln_var[:, cs], ln_var[:, cs]),
                   deps=[s8], tag=f"{pre}rv{tt}")
        s10 = P.add("scalar", lambda s, cs=cs: s.activation(ln_rstd[:, cs], ln_var[:, cs],
                                                            AF.Sqrt),
                    deps=[s9], tag=f"{pre}rstd{tt}")
        s11 = P.add("vector", lambda v, cs=cs: v.scalar_tensor_tensor(
                        ln_nmrs[:, cs], ln_mean[:, cs], -1.0, ln_rstd[:, cs],
                        op0=ALU.mult, op1=ALU.mult),
                    deps=[s10, s4, ld_constv], tag=f"{pre}nmrs{tt}")
        return s11

    ln_chain.prev_s3 = None

    def ln1_tile(tt):
        dep0 = [ev_add[3 * NTT + tt]]
        accs = acc[:, tt * D:(tt + 1) * D]
        s11 = ln_chain(tt, dep0, accs, "ln_")
        s12 = P.add("scalar", lambda s, a=accs, tt=tt: s.activation(
                        a, a, AF.Identity, bias=ln_nmrs[:, tt:tt + 1],
                        scale=ln_rstd[:, tt:tt + 1]),
                    deps=[s11], tag=f"ln_app{tt}")
        ln_done[tt] = s12
        s13a = P.add("scalar", lambda s, a=accs, tt=tt: s.copy(
                         x1bf[tt % 2][:, 0:1024], a[:, 0:1024]),
                     deps=[s12, x1Tb_d.get(tt - 2)], tag=f"x1bfa{tt}")
        s13b = P.add("vector", lambda v, a=accs, tt=tt: v.tensor_copy(
                         x1bf[tt % 2][:, 1024:2048], a[:, 1024:2048]),
                     deps=[s12, x1Tb_d.get(tt - 2)], tag=f"x1bfb{tt}")
        def mk_x1T(tt, half):
            def f(t):
                last = None
                for dc in range(half * 8, half * 8 + 8):
                    last = t.transpose(pb[half][0:128, (dc % 8) * 128:(dc % 8 + 1) * 128],
                                       x1bf[tt % 2][:, dc * 128:(dc + 1) * 128], ident[:])
                return last
            return f
        x1Ta = P.add("tensor", mk_x1T(tt, 0),
                     deps=[s13a, x1eva.get(tt - 1), ld_consts[0],
                           a_pTev.get(NIT - 1) if tt == 0 else None], tag=f"x1Ta{tt}")
        x1Tb = P.add("tensor", mk_x1T(tt, 1),
                     deps=[s13b, x1evb.get(tt - 1),
                           a_pTev.get(NIT - 1) if tt == 0 else None], tag=f"x1Tb{tt}")
        x1Tb_d[tt] = x1Tb
        def mk_x1ev(tt, half):
            def f(v):
                return v.tensor_copy(
                    x1_fm[:].rearrange("p (c t) -> p c t", c=NDC)
                    [:, half * 8:half * 8 + 8, tt * 128:(tt + 1) * 128],
                    pb[half][0:128, 0:1024].rearrange("p (c t) -> p c t", c=8))
            return f
        x1eva[tt] = P.add("vector", mk_x1ev(tt, 0),
                          deps=[x1Ta, last_att_te if tt == 0 else None],
                          tag=f"x1eva{tt}")
        x1evb[tt] = P.add("vector", mk_x1ev(tt, 1), deps=[x1Tb], tag=f"x1evb{tt}")

    def gate_tile(tt):
        def mk_gmm(tt=tt):
            def f(t):
                last = None
                for dc in range(NDC):
                    xs = x1_fm[:, dc * TOK + tt * 128: dc * TOK + tt * 128 + 128]
                    t.ldweights(xs)
                    last = t.matmul(pf[4][:, 0:8], xs, gateT[:, dc * 8:(dc + 1) * 8],
                                    start=(dc == 0), stop=(dc == NDC - 1))
                return last
            return f
        gmm[tt] = P.add("tensor", mk_gmm(),
                        deps=[x1eva[tt], x1evb[tt], ld_consts[6], g1p[0]],
                        tag=f"gmm{tt}")
        g8 = slice(tt * 8, tt * 8 + 8)
        ct = slice(tt, tt + 1)
        g1 = P.add("scalar", lambda s, g8=g8: s.activation(sc_gate[:, g8], pf[4][:, 0:8],
                                                           AF.Sigmoid),
                   deps=[gmm[tt]], tag=f"gsig{tt}")
        g1p[0] = g1
        g2 = P.add("vector", lambda v, g8=g8, ct=ct: v.reduce_max(
                        g_max1[:, ct], sc_gate[:, g8], axis=AX.X),
                   deps=[g1], tag=f"gm1{tt}")
        g3 = P.add("vector", lambda v, g8=g8, ct=ct: v.tensor_scalar(
                        g_eq[:, g8], sc_gate[:, g8],
                        g_max1[:, ct], None, op0=ALU.is_equal),
                   deps=[g2], tag=f"geq{tt}")
        g4 = P.add("vector", lambda v, g8=g8: v.scalar_tensor_tensor(
                        g_msk[:, g8], g_eq[:, g8], NEG, sc_gate[:, g8],
                        op0=ALU.mult, op1=ALU.add),
                   deps=[g3, ld_constv], tag=f"gmsk{tt}")
        g5 = P.add("vector", lambda v, g8=g8, ct=ct: v.reduce_max(
                        g_max2[:, ct], g_msk[:, g8], axis=AX.X),
                   deps=[g4], tag=f"gm2{tt}")
        g6 = P.add("vector", lambda v, g8=g8, ct=ct: v.tensor_scalar(
                        g_ge[:, g8], sc_gate[:, g8],
                        g_max2[:, ct], None, op0=ALU.is_ge),
                   deps=[g5], tag=f"gge{tt}")
        g7 = P.add("vector", lambda v, tt=tt, g8=g8: v.tensor_tensor(
                        m_sb[:, tt * 9:tt * 9 + 8], sc_gate[:, g8], g_ge[:, g8],
                        op=ALU.mult),
                   deps=[g6, op_mset], tag=f"gm{tt}")
        gate_m[tt] = g7

    w13_ld = {}
    w2_ld = {}
    ab_mm = {}
    h_silu = {}
    h_stt = {}
    h_T = {}
    h_ev = {}
    w2_mm = {}
    acc_add = {}
    lastw2 = {}

    def abX_tile(hq, tt):
        slot = w13_ring[hq % 2]
        bank = pf[tt % 4]
        def mk_ab(slot=slot, tt=tt, bank=bank):
            def f(t):
                last = None
                for dc in range(NDC):
                    xs = x1_fm[:, dc * TOK + tt * 128: dc * TOK + tt * 128 + 128]
                    t.ldweights(xs)
                    last = t.matmul(bank[:, 0:512], xs,
                                    slot[:, dc * 512:(dc + 1) * 512],
                                    start=(dc == 0), stop=(dc == NDC - 1))
                return last
            return f
        prev = h_stt.get((0, 0, tt - 4))
        deps = [w13_ld[(0, hq, 1)],
                x1eva.get(tt), x1evb.get(tt), prev,
                ev_add.get(4 * NTT - 1) if tt < 4 else None]
        ab_mm[(hq, tt)] = P.add("tensor", mk_ab(), deps=deps, tag=f"ab0,{hq},{tt}")
        hcol = tt * HID + hq * 256
        def mk_silu(tt=tt, bank=bank, hcol=hcol):
            def f(s):
                return s.activation(h_tm[:, hcol:hcol + 256], bank[:, 0:256],
                                    AF.Silu, bias=0.0,
                                    scale=m_sb[:, tt * 9: tt * 9 + 1])
            return f
        h_silu[(0, hq, tt)] = P.add(
            "scalar", mk_silu(),
            deps=[ab_mm[(hq, tt)], gate_m[tt], wo_mm[(3, NTT - 1)]],
            tag=f"silu0,{hq},{tt}")
        def mk_stt(tt=tt, bank=bank, hcol=hcol):
            def f(v):
                return v.scalar_tensor_tensor(
                    h_tm[:, hcol:hcol + 256], bank[:, 256:512],
                    m_sb[:, tt * 9: tt * 9 + 1],
                    h_tm[:, hcol:hcol + 256], op0=ALU.mult, op1=ALU.mult)
            return f
        h_stt[(0, hq, tt)] = P.add("vector", mk_stt(),
                                   deps=[h_silu[(0, hq, tt)]],
                                   tag=f"hstt0,{hq},{tt}")

    wo_load(0, [last_att_te])
    wo_load(1, [last_att_te])
    utm_issue(0)
    for tt in range(NTT):
        wo_tile(0, tt)
    utm_issue(1)
    wo_load(2, [wo_mm[(0, NTT - 1)]])
    for tt in range(NTT):
        wo_tile(1, tt)
    utm_issue(2)
    wo_load(3, [wo_mm[(1, NTT - 1)]])
    for tt in range(NTT):
        wo_tile(2, tt)
    utm_issue(3)
    for tt in range(NTT):
        wo_tile(3, tt)
    for hq in (0,):
        op = gload(w13_ring[hq][:],
                   d_w13[0][:, hq * NDC * 512:(hq + 1) * NDC * 512], f"w13{hq}",
                   deps=[wo_mm[(3, NTT - 1)]], tag=f"w13L0,{hq}")
        w13_ld[(0, hq, 1)] = w13_ld[(0, hq, 3)] = op
    ln1_tile(0)
    ln1_tile(1)
    for tt in range(NTT):
        if tt + 2 < NTT:
            ln1_tile(tt + 2)
        gate_tile(tt)
        abX_tile(0, tt)

    for e in range(NE + 1):
        s13d = d_w13[e]
        s2d = d_w2[e]
        for hq in range(4):
            li = e * 4 + hq
            if (e, hq, 1) in w13_ld:
                continue
            slot = w13_ring[li % 2]
            dep = [ab_mm.get((li - 2, NTT - 1)),
                   wo_mm.get((3, NTT - 1)) if li < 2 else None]
            op = gload(slot[:],
                       s13d[:, hq * NDC * 512:(hq + 1) * NDC * 512],
                       f"w13{li % 2}",
                       deps=dep, tag=f"w13L{e},{hq}")
            w13_ld[(e, hq, 1)] = w13_ld[(e, hq, 3)] = op
        for dq in range(4):
            li = e * 4 + dq
            slot = w2_ring[li % 2]
            dep = [w2_mm.get(((li - 2) // 4, (li - 2) % 4, NTT - 1))
                   if li >= 2 and (li - 2) // 4 < e else None,
                   ev_add.get(4 * NTT - 1) if li < 2 else None]
            w2_ld[(e, dq)] = gload(
                slot[:],
                s2d[:, dq * NHC * 512:(dq + 1) * NHC * 512],
                f"w2{li % 2}",
                deps=dep, tag=f"w2L{e},{dq}")

        for hq in range(4):
            li = e * 4 + hq
            slot = w13_ring[li % 2]
            for tt in range(NTT):
                bank = pf[tt % 4]
                def mk_ab(slot=slot, tt=tt, bank=bank):
                    def f(t):
                        last = None
                        for dc in range(NDC):
                            xs = x1_fm[:, dc * TOK + tt * 128: dc * TOK + tt * 128 + 128]
                            t.ldweights(xs)
                            last = t.matmul(bank[:, 0:512], xs,
                                            slot[:, dc * 512:(dc + 1) * 512],
                                            start=(dc == 0), stop=(dc == NDC - 1))
                        return last
                    return f
                if (li, tt) not in ab_mm:
                    prev_stt = (h_stt.get((e, hq, tt - 4)) if tt >= 4 else
                                h_stt.get((e, hq - 1, NTT - 4 + tt)) if hq > 0 else
                                h_stt.get((e - 1, 3, NTT - 4 + tt)))
                    deps = [w13_ld[(e, hq, 1)], prev_stt]
                    ab_mm[(li, tt)] = P.add("tensor", mk_ab(), deps=deps,
                                            tag=f"ab{e},{hq},{tt}")
                if (e, hq, tt) in h_silu:
                    continue
                hcol = tt * HID + hq * 256
                def mk_silu(e=e, tt=tt, bank=bank, hcol=hcol):
                    def f(s):
                        return s.activation(h_tm[:, hcol:hcol + 256], bank[:, 0:256],
                                            AF.Silu, bias=0.0,
                                            scale=m_sb[:, tt * 9 + e: tt * 9 + e + 1])
                    return f
                deps = [ab_mm[(li, tt)], gate_m[tt] if e < NE else op_mset,
                        h_T.get((e - 1, tt)),
                        wo_mm.get((3, NTT - 1)) if e == 0 else None,
                        w2_mm.get((e - 1 if e > 0 else 0, NTT - 1)) if e == 1 else None]
                h_silu[(e, hq, tt)] = P.add("scalar", mk_silu(), deps=deps,
                                            tag=f"silu{e},{hq},{tt}")
                def mk_stt(e=e, tt=tt, bank=bank, hcol=hcol):
                    def f(v):
                        return v.scalar_tensor_tensor(
                            h_tm[:, hcol:hcol + 256], bank[:, 256:512],
                            m_sb[:, tt * 9 + e: tt * 9 + e + 1],
                            h_tm[:, hcol:hcol + 256], op0=ALU.mult, op1=ALU.mult)
                    return f
                h_stt[(e, hq, tt)] = P.add("vector", mk_stt(),
                                           deps=[h_silu[(e, hq, tt)]],
                                           tag=f"hstt{e},{hq},{tt}")
        for hq in (2, 3):
            gate = ab_mm[(e * 4 + hq - 2, NTT - 1)]
            P.add_dep(w13_ld[(e, hq, 1)], gate)
            P.add_dep(w13_ld[(e, hq, 3)], gate)
        for tt in range(NTT):
            def mk_hT(e=e, tt=tt):
                def f(t):
                    last = None
                    for hc in range(8):
                        last = t.transpose(pb[tt % 2][0:128, hc * 128:(hc + 1) * 128],
                                           h_tm[:, tt * HID + hc * 128: tt * HID + (hc + 1) * 128],
                                           ident[:])
                    return last
                return f
            deps = [h_stt[(e, hq2, tt)] for hq2 in range(4)]
            deps += [h_ev.get((e, tt - 2)) if tt >= 2
                     else h_ev.get((e - 1, NTT - 2 + tt)), ld_consts[0]]
            if e == 0 and tt < 2:
                deps += [x1eva.get(NTT - 1), x1evb.get(NTT - 1)]
            h_T[(e, tt)] = P.add("tensor", mk_hT(), deps=deps, tag=f"hT{e},{tt}")
            def mk_hev(e=e, tt=tt):
                def f(v):
                    return v.tensor_copy(
                        h_fm[:].rearrange("p (c t) -> p c t", c=NHC)[:, :, tt * 128:(tt + 1) * 128],
                        pb[tt % 2][0:128, 0:1024].rearrange("p (c t) -> p c t", c=8))
                return f
            h_ev[(e, tt)] = P.add("vector", mk_hev(),
                                  deps=[h_T[(e, tt)],
                                        w2_mm.get((e - 1, tt)),
                                        wo_mm.get((3, NTT - 1)) if e == 0 else None],
                                  tag=f"hev{e},{tt}")
        for dq in range(4):
            li2 = e * 4 + dq
            for tt in range(NTT):
                bank = pf[4 + (dq * NTT + tt) % 2]
                def mk_w2(e=e, dq=dq, tt=tt, bank=bank, li2=li2):
                    def f(t):
                        last = None
                        for hc in range(NHC):
                            hs = h_fm[:, hc * TOK + tt * 128: hc * TOK + tt * 128 + 128]
                            t.ldweights(hs)
                            last = t.matmul(bank[:, 0:512], hs,
                                            w2_ring[li2 % 2][:, hc * 512:(hc + 1) * 512],
                                            start=(hc == 0), stop=(hc == NHC - 1))
                        return last
                    return f
                deps = [w2_ld[(e, dq)], h_ev[(e, tt)],
                        acc_add.get((e, dq, tt - 2) if tt >= 2 else
                                    (e, dq - 1, NTT - 2 + tt) if dq > 0 else
                                    (e - 1, 3, NTT - 2 + tt))]
                op = P.add("tensor", mk_w2(), deps=deps, tag=f"w2{e},{dq},{tt}")
                if dq == 3:
                    w2_mm[(e, tt)] = op
                w2_mm[(e, dq, tt)] = op
                def mk_acc(e=e, dq=dq, tt=tt, bank=bank):
                    def f(v):
                        a = acc[:, tt * D + dq * 512: tt * D + (dq + 1) * 512]
                        return v.tensor_tensor(a, bank[:, 0:512], a, op=ALU.add)
                    return f
                acc_add[(e, dq, tt)] = P.add(
                    "vector", mk_acc(),
                    deps=[w2_mm[(e, dq, tt)],
                          ln_done[tt] if e == 0 else None], tag=f"acc{e},{dq},{tt}")
        for dq in (2, 3):
            P.add_dep(w2_ld[(e, dq)], w2_mm[(e, dq - 2, NTT - 1)])
        lastw2[e] = w2_mm[(e, 3, NTT - 1)]

    st_ops = {}
    for tt in range(NTT):
        accs = acc[:, tt * D:(tt + 1) * D]
        dep0 = [acc_add[(NE, 3, tt)]] + [acc_add[(NE, dqq, tt)] for dqq in range(4)]
        s11 = ln_chain(tt, dep0, accs, "l2_")
        s12 = P.add("scalar", lambda s, a=accs, tt=tt: s.activation(
                        out_ring[tt % 2][:], a, AF.Identity, bias=ln_nmrs[:, tt:tt + 1],
                        scale=ln_rstd[:, tt:tt + 1]),
                    deps=[s11, st_ops.get(tt - 2),
                          lastw2[NE] if tt == 0 else None], tag=f"l2app{tt}")
        def mk_st(tt=tt):
            def f(s):
                return s.dma_start(out=d_out[tt * 128:(tt + 1) * 128, :],
                                   in_=out_ring[tt % 2][:])
            return f
        st_ops[tt] = P.add("sync", mk_st(), deps=[s12], dma=True,
                           skey=f"out{tt % 2}", tag=f"st{tt}")

    P.finalize(nc)

    with nc.Block() as block:
        @block.tensor
        def _(t):
            P.emit_engine("tensor", t)

        @block.vector
        def _(v):
            P.emit_engine("vector", v)

        @block.scalar
        def _(s):
            P.emit_engine("scalar", s)

        @block.gpsimd
        def _(g):
            P.emit_engine("gpsimd", g)

        @block.sync
        def _(s):
            P.emit_engine("sync", s)
            s.wait_ge(P.sems["out0"], P.final_counts["out0"])
            s.wait_ge(P.sems["out1"], P.final_counts["out1"])
    return nc


def _rope_tables(FT):
    import jax
    import jax.numpy as jnp
    cpu = jax.devices("cpu")[0]
    with jax.default_device(cpu):
        half = 32
        inv_freq = 1.0 / 10000.0 ** (jnp.arange(half, dtype=jnp.float32) / half)
        ang = jnp.arange(8192, dtype=jnp.float32)[:, None] * inv_freq
        cos = np.asarray(jnp.cos(ang), dtype=np.float32).astype(FT)
        sin = np.asarray(jnp.sin(ang), dtype=np.float32).astype(FT)
    return cos, sin


def _band_masks(core, ncores, nqt, FT):
    m = np.zeros((3, 128, 272), dtype=FT)
    r = np.arange(128)[:, None]
    j = np.arange(256)[None, :]
    band = np.where((j >= r) & (j <= r + 128), 0.0, NEG).astype(FT)
    for t in range(3):
        m[t, :, 16:] = band
    if core == 0:
        m[0, :, 16:16 + 64] = NEG
    if core == ncores - 1:
        m[2, :, 16 + 192:] = NEG
    return m


def _fm_chunks(WT, cpc):
    K, N = WT.shape
    c = K // 128
    nq = N // cpc
    a = WT.reshape(c, 128, nq, cpc).transpose(1, 2, 0, 3)
    return np.ascontiguousarray(a.reshape(128, nq * c * cpc))


def prep_inmaps(L0_attn_u, L0_attn_Wqkv, L0_attn_bqkv, L0_attn_Wo, L0_attn_bo,
                L0_attn_sinks, L0_ffn_gate_w, L0_ffn_w1, L0_ffn_w2, L0_ffn_w3,
                L0_ffn_ws1, L0_ffn_ws2, L0_ffn_ws3):
    FT = np.float64 if np.asarray(L0_attn_u).dtype == np.float64 else np.float32
    u = np.ascontiguousarray(np.asarray(L0_attn_u, dtype=FT)[0])
    L = u.shape[0]
    TOK = L // NCORES
    TH = TOK + 128
    NQT = TOK // 128

    wqkvT = np.ascontiguousarray(np.asarray(L0_attn_Wqkv, FT).T)
    wq_h = _fm_chunks(wqkvT[:, 0:D], 128)
    wk_h = _fm_chunks(wqkvT[:, D:2 * D], 128)
    wv_h = _fm_chunks(wqkvT[:, 2 * D:3 * D], 512)
    bqkv = np.asarray(L0_attn_bqkv, FT)
    bq = np.ascontiguousarray(bqkv[0:D].reshape(NDC, 128).T)
    bk = np.ascontiguousarray(bqkv[D:2 * D].reshape(NDC, 128).T)
    bv_row = np.ascontiguousarray(bqkv[2 * D:3 * D].reshape(1, D))
    wo_h = _fm_chunks(np.asarray(L0_attn_Wo, FT).T, 512)
    bo_row = np.ascontiguousarray(np.asarray(L0_attn_bo, FT).reshape(1, D))
    sinks = np.asarray(L0_attn_sinks, FT)
    sinksN = np.ascontiguousarray(sinks.transpose(1, 0, 2).reshape(NSINK, NH * DH))
    sT = sinks.transpose(0, 2, 1).reshape(NH, DH, NSINK)
    sinksT = np.zeros((128, NH * NSINK), dtype=FT)
    for h in range(NH):
        sinksT[0:64, h * NSINK:(h + 1) * NSINK] = sT[h]
        sinksT[64:128, h * NSINK:(h + 1) * NSINK] = sT[h]
    gate_h = _fm_chunks(np.asarray(L0_ffn_gate_w, FT).T, NE)
    w1T = np.asarray(L0_ffn_w1, FT).transpose(0, 2, 1)
    w3T = np.asarray(L0_ffn_w3, FT).transpose(0, 2, 1)
    w2T = np.asarray(L0_ffn_w2, FT).transpose(0, 2, 1)

    def _w13_block(w1m, w3m):
        blocks = []
        for hq in range(4):
            cat = np.concatenate([w1m[:, hq * 256:(hq + 1) * 256],
                                  w3m[:, hq * 256:(hq + 1) * 256]], axis=1)
            blocks.append(_fm_chunks(cat, 512))
        return np.concatenate(blocks, axis=1)

    w13_h = np.stack([_w13_block(w1T[e], w3T[e]) for e in range(NE)]
                     + [_w13_block(np.asarray(L0_ffn_ws1, FT).T,
                                   np.asarray(L0_ffn_ws3, FT).T)])
    w2_h = np.stack([_fm_chunks(w2T[e], 512) for e in range(NE)]
                    + [_fm_chunks(np.asarray(L0_ffn_ws2, FT).T, 512)])

    constv = np.tile(np.array([1.0 / D, 1e-5, -1.0, NEG], dtype=FT), (128, 1))
    ident = np.eye(128, dtype=FT)
    perm = np.zeros((128, 128), dtype=FT)
    for b in range(4):
        i = np.arange(32)
        perm[b * 32 + i, (b ^ 1) * 32 + i] = 1.0
    cos, sin = _rope_tables(FT)

    in_maps = []
    for c in range(NCORES):
        lo, hi = c * TOK - 64, c * TOK + TOK + 64
        upad = np.zeros((TH, D), dtype=FT)
        s0, s1 = max(lo, 0), min(hi, L)
        upad[s0 - lo: s1 - lo] = u[s0:s1]
        u_fm_h = _fm_chunks(np.ascontiguousarray(upad.T), TH)
        pos = np.clip(np.arange(lo, hi), 0, L - 1)
        ck = cos[pos].T
        sk = sin[pos].T
        Ck = np.ascontiguousarray(np.tile(ck, (4, 1)))
        Sk = np.ascontiguousarray(np.concatenate([-sk, sk, -sk, sk], axis=0))
        Cq = np.ascontiguousarray(Ck[:, 64:64 + TOK] * FT(0.125))
        Sq = np.ascontiguousarray(Sk[:, 64:64 + TOK] * FT(0.125))
        masks_h = np.ascontiguousarray(
            _band_masks(c, NCORES, NQT, FT).transpose(1, 0, 2).reshape(128, 3 * 272))
        in_maps.append({
            "u_fm_d": u_fm_h,
            "u_tm": np.ascontiguousarray(u[c * TOK:(c + 1) * TOK]),
            "wq_d": wq_h, "wk_d": wk_h, "wv_d": wv_h,
            "bq": bq, "bk": bk, "bv_row": bv_row,
            "wo_d": wo_h, "bo_row": bo_row, "sinksN": sinksN, "sinksT": sinksT,
            "Cq": Cq, "Sq": Sq, "Ck": Ck, "Sk": Sk,
            "masks": masks_h,
            "ident": ident, "perm": perm, "gate_d": gate_h,
            "w13_d": w13_h, "w2_d": w2_h,
            "constv": constv,
        })
    return in_maps, TOK, L, FT


def kernel(**inputs):
    in_maps, TOK, L, FT = prep_inmaps(**inputs)
    nc = build_program(TOK)
    res = run_bass_kernel_spmd(nc, in_maps, core_ids=list(range(NCORES)))
    out = np.concatenate([np.asarray(r["y_out"]) for r in res.results], axis=0)
    return out.reshape(1, L, D).astype(FT)


# version 22 (final)
# speedup vs baseline: 1.0162x; 1.0162x over previous
import math
import numpy as np
import ml_dtypes

import concourse.bass as bass
import concourse.mybir as mybir
from concourse.bass_utils import run_bass_kernel_spmd

F32 = mybir.dt.float32
BF16 = mybir.dt.bfloat16
AF = mybir.ActivationFunctionType
ALU = mybir.AluOpType
AX = mybir.AxisListType

NCORES = 8
D = 2048
NH = 32
DH = 64
NSINK = 16
HID = 1024
NE = 8
NDC = D // 128
NHC = HID // 128
NEG = -1.0e30

BPP = 212_800


class Op:
    __slots__ = ("i", "eng", "emit", "deps", "dma", "skey", "needed", "val", "tag")

    def __init__(self, i, eng, emit, deps, dma, skey, tag):
        self.i, self.eng, self.emit, self.deps, self.dma, self.skey, self.tag = (
            i, eng, emit, list(deps), dma, skey, tag)
        self.needed = False
        self.val = None


class Plan:

    def __init__(self, one_shot=()):
        self.ops = []
        self.one_shot = set(one_shot)

    def add(self, eng, emit, deps=(), dma=False, skey=None, tag=""):
        deps = [d for d in deps if d is not None]
        op = Op(len(self.ops), eng, emit, deps, dma,
                skey if dma else eng, tag)
        assert not dma or skey is not None
        self.ops.append(op)
        return op

    def add_dep(self, op, dep):
        if dep is not None:
            op.deps.append(dep)

    def _needs_wait(self, consumer, dep):
        return True

    def finalize(self, nc):
        for op in self.ops:
            if op.dma:
                op.needed = True
            for d in op.deps:
                if self._needs_wait(op, d):
                    d.needed = True
        cnt = {}
        for op in self.ops:
            if op.needed:
                cnt[op.skey] = cnt.get(op.skey, 0) + (16 if op.dma else 1)
                op.val = cnt[op.skey]
        for op in self.ops:
            if op.needed and op.skey in self.one_shot:
                op.val = cnt[op.skey]
        self.final_counts = cnt
        self.sems = {k: nc.alloc_semaphore("s_" + k) for k in cnt}
        from collections import deque
        succ = {}
        indeg = [0] * len(self.ops)
        last_on = {}
        for op in self.ops:
            if op.eng in last_on:
                succ.setdefault(last_on[op.eng].i, []).append(op.i)
                indeg[op.i] += 1
            last_on[op.eng] = op
            for d in op.deps:
                succ.setdefault(d.i, []).append(op.i)
                indeg[op.i] += 1
        q = deque(i for i in range(len(self.ops)) if indeg[i] == 0)
        seen = 0
        while q:
            i = q.popleft()
            seen += 1
            for j in succ.get(i, ()):
                indeg[j] -= 1
                if indeg[j] == 0:
                    q.append(j)
        assert seen == len(self.ops), f"dependency cycle: {len(self.ops) - seen} ops"

    def emit_engine(self, eng_name, eng):
        watermark = {}
        for op in self.ops:
            if op.eng != eng_name:
                continue
            want = {}
            for d in op.deps:
                if not self._needs_wait(op, d):
                    continue
                if d.val is None:
                    continue
                if max(watermark.get(d.skey, 0), want.get(d.skey, 0)) >= d.val:
                    continue
                want[d.skey] = d.val
            for k, v in want.items():
                watermark[k] = v
                eng.wait_ge(self.sems[k], v)
            inst = op.emit(eng)
            if op.needed:
                inst.then_inc(self.sems[op.skey], 16 if op.dma else 1)


def build_program(TOK):
    TH = TOK + 128
    NQT = TOK // 128
    NTT = TOK // 128
    NVT = TH // 128

    nc = bass.Bass("TRN2", target_bir_lowering=False, debug=False)

    d_ufm = nc.dram_tensor("u_fm_d", [128, NDC * TH], F32, kind="ExternalInput").ap()
    d_utm = nc.dram_tensor("u_tm", [TOK, D], F32, kind="ExternalInput").ap()
    d_wq = nc.dram_tensor("wq_d", [128, NDC * NDC * 128], F32, kind="ExternalInput").ap()
    d_wk = nc.dram_tensor("wk_d", [128, NDC * NDC * 128], F32, kind="ExternalInput").ap()
    d_wv = nc.dram_tensor("wv_d", [128, 4 * NDC * 512], F32, kind="ExternalInput").ap()
    d_bq = nc.dram_tensor("bq", [128, NDC], F32, kind="ExternalInput").ap()
    d_bk = nc.dram_tensor("bk", [128, NDC], F32, kind="ExternalInput").ap()
    d_bvrow = nc.dram_tensor("bv_row", [1, D], F32, kind="ExternalInput").ap()
    d_wo = nc.dram_tensor("wo_d", [128, 4 * NDC * 512], F32, kind="ExternalInput").ap()
    d_borow = nc.dram_tensor("bo_row", [1, D], F32, kind="ExternalInput").ap()
    d_sinksN = nc.dram_tensor("sinksN", [NSINK, NH * DH], F32, kind="ExternalInput").ap()
    d_sinksT = nc.dram_tensor("sinksT", [128, NH * NSINK], F32, kind="ExternalInput").ap()
    d_Cq = nc.dram_tensor("Cq", [128, TOK], F32, kind="ExternalInput").ap()
    d_Sq = nc.dram_tensor("Sq", [128, TOK], F32, kind="ExternalInput").ap()
    d_Ck = nc.dram_tensor("Ck", [128, TH], F32, kind="ExternalInput").ap()
    d_Sk = nc.dram_tensor("Sk", [128, TH], F32, kind="ExternalInput").ap()
    d_masks = nc.dram_tensor("masks", [128, 3 * 272], F32, kind="ExternalInput").ap()
    d_ident = nc.dram_tensor("ident", [128, 128], F32, kind="ExternalInput").ap()
    d_perm = nc.dram_tensor("perm", [128, 128], F32, kind="ExternalInput").ap()
    d_gate = nc.dram_tensor("gate_d", [128, NDC * NE], F32, kind="ExternalInput").ap()
    d_w13 = nc.dram_tensor("w13_d", [NE + 1, 128, 4 * NDC * 512], F32,
                           kind="ExternalInput").ap()
    d_w2 = nc.dram_tensor("w2_d", [NE + 1, 128, 4 * NHC * 512], F32, kind="ExternalInput").ap()
    d_constv = nc.dram_tensor("constv", [128, 4], F32, kind="ExternalInput").ap()
    d_out = nc.dram_tensor("y_out", [TOK, D], F32, kind="ExternalOutput").ap()

    BASE = (int(nc._init_sbuf_base) + 31) // 32 * 32

    def sb(name, off, shape, dtype):
        return nc.alloc_sbuf_tensor_at(name, shape, dtype, offset=BASE + off)

    o = 0
    masks = sb("masks", o, [128, 3 * 272], F32); o += 3264
    ident = sb("ident", o, [128, 128], BF16); o += 256
    perm = sb("perm", o, [128, 128], BF16); o += 256
    sinksT = sb("sinksT", o, [128, NH * NSINK], BF16); o += 1024
    sinksN = sb("sinksN", o, [NSINK, NH * DH], BF16); o += 4096
    bq = sb("bq", o, [128, NDC], F32); o += 64
    bk = sb("bk", o, [128, NDC], F32); o += 64
    borow = sb("borow", o, [1, D], BF16); o += 4096
    bvrow = sb("bvrow", o, [1, D], BF16); o += 4096
    ones = sb("ones", o, [1, 128], BF16); o += 256
    gateT = sb("gateT", o, [128, NDC * NE], BF16); o += 256
    m_sb = sb("m_sb", o, [128, NTT * 9], F32); o += 4 * NTT * 9
    o = (o + 31) // 32 * 32
    negmax = sb("negmax", o, [128, 4], F32); o += 32
    rowsum = sb("rowsum", o, [128, 4], F32); o += 32
    recipv = sb("recipv", o, [128, 4], F32); o += 32
    ln_s = sb("ln_s", o, [128, 8], F32); o += 32
    ln_ss = sb("ln_ss", o, [128, 8], F32); o += 32
    ln_mean = sb("ln_mean", o, [128, 8], F32); o += 32
    ln_msq = sb("ln_msq", o, [128, 8], F32); o += 32
    ln_var = sb("ln_var", o, [128, 8], F32); o += 32
    ln_rstd = sb("ln_rstd", o, [128, 8], F32); o += 32
    ln_nmrs = sb("ln_nmrs", o, [128, 8], F32); o += 32
    sc_gate = sb("sc_gate", o, [128, 64], F32); o += 256
    g_max1 = sb("g_max1", o, [128, 8], F32); o += 32
    g_eq = sb("g_eq", o, [128, 64], F32); o += 256
    g_msk = sb("g_msk", o, [128, 64], F32); o += 256
    g_max2 = sb("g_max2", o, [128, 8], F32); o += 32
    g_ge = sb("g_ge", o, [128, 64], F32); o += 256
    constv = sb("constv", o, [128, 4], F32); o += 32
    nc.const_aps.register(F32, 1.0 / D, constv[:, 0:1])
    nc.const_aps.register(F32, 1e-5, constv[:, 1:2])
    nc.const_aps.register(F32, -1.0, constv[:, 2:3])
    nc.const_aps.register(F32, NEG, constv[:, 3:4])
    FIXED_END = (o + 31) // 32 * 32

    A0 = FIXED_END
    u_fm = sb("u_fm", A0, [128, NDC * TH], BF16)
    y_fm = sb("y_fm", A0, [128, NDC * TOK], BF16)
    x1bf0 = sb("x1bf0", A0 + NDC * TOK * 2, [128, D], BF16)
    h_tm = sb("h_tm", A0, [128, NTT * HID], BF16)
    h_fm = sb("h_fm", A0 + NTT * HID * 2, [128, NHC * TOK], BF16)
    szA = NDC * TH * 2

    B0 = A0 + szA
    CS_OFF = B0
    Cq_sb = sb("Cq", CS_OFF, [128, TOK], F32)
    Sq_sb = sb("Sq", CS_OFF + TOK * 4, [128, TOK], F32)
    Ck_sb = sb("Ck", CS_OFF + TOK * 8, [128, TH], F32)
    Sk_sb = sb("Sk", CS_OFF + TOK * 8 + TH * 4, [128, TH], F32)
    WQ_OFF = CS_OFF + TOK * 8 + TH * 8
    wq_ring = [sb(f"wqr{i}", WQ_OFF + i * NDC * 128 * 2, [128, NDC * 128], BF16)
               for i in range(2)]
    RP_OFF = WQ_OFF + 2 * NDC * 128 * 2
    qraw = [sb(f"qraw{i}", RP_OFF + i * 1024, [128, 512], BF16) for i in range(2)]
    t1r = [sb(f"t1r{i}", RP_OFF + 2048 + i * 1024, [128, 512], BF16) for i in range(2)]
    t2r = [sb(f"t2r{i}", RP_OFF + 4096 + i * 1024, [128, 512], BF16) for i in range(2)]
    QF_OFF = RP_OFF + 6144
    q_fm = sb("q_fm", QF_OFF, [128, NDC * TOK], BF16)
    wv_ring = [sb(f"wvr{i}", QF_OFF + i * NDC * 512 * 2, [128, NDC * 512], BF16)
               for i in range(2)]
    szB = QF_OFF + NDC * TOK * 2 - B0
    acc = sb("acc", B0, [128, NTT * D], F32)
    assert NTT * D * 4 <= szB + 2048, (NTT * D * 4, szB)
    szB = max(szB, NTT * D * 4)

    C0 = B0 + szB
    k_fm = sb("k_fm", C0, [128, NDC * TH], BF16)
    x1_fm = sb("x1_fm", C0, [128, NDC * TOK], BF16)
    sqs = sb("sqs", C0 + NDC * TOK * 2, [128, D], BF16)
    szC = NDC * TH * 2

    D0 = C0 + szC
    v_tm = sb("v_tm", D0, [128, NVT * D], BF16)
    wo_ring = [sb(f"wor{i}", D0 + i * NDC * 512 * 2, [128, NDC * 512], BF16)
               for i in range(2)]
    w13_ring = [sb(f"w13r{i}", D0 + i * NDC * 512 * 2, [128, NDC * 512], BF16)
                for i in range(2)]
    x1bf1 = sb("x1bf1", D0 + 2 * NDC * 512 * 2, [128, D], BF16)
    x1bf = [x1bf0, x1bf1]
    szD = NVT * D * 2

    E0 = D0 + szD
    S_sb = [sb(f"Ssb{i}", E0 + i * 1088, [128, 272], F32) for i in range(4)]
    p_sb = [sb(f"psb{i}", E0 + 4352 + i * 544, [128, 272], BF16) for i in range(4)]
    pn_sb = [sb(f"pnsb{i}", E0 + 6528 + i * 544, [128, 272], BF16) for i in range(4)]
    pT_sb = [sb(f"pTsb{i}", E0 + 8704 + i * 768, [128, 384], BF16) for i in range(4)]
    utm_ring = [sb(f"utmr{i}", E0 + i * 2048, [128, 512], F32) for i in range(8)]
    w2_ring = [sb(f"w2r{i}", E0 + i * NHC * 512 * 2, [128, NHC * 512], BF16)
               for i in range(2)]
    out_ring = [sb(f"outr{i}", E0 + i * D * 4, [128, D], F32) for i in range(2)]
    szE = 16384
    assert BASE + E0 + szE <= 229376, (BASE, E0 + szE)

    pf = [nc.alloc_psum_tensor(f"pf{i}", [128, 512], F32) for i in range(6)]
    pb = [nc.alloc_psum_tensor(f"pb{i}", [128, 1024], BF16) for i in range(2)]

    P = Plan(one_shot=("c0", "c1"))

    def gload(dst_ap, src_ap, skey, deps=(), tag="gload"):
        return P.add("gpsimd", lambda g, d=dst_ap, s=src_ap: g.dma_start(out=d, in_=s),
                     deps=deps, dma=True, skey=skey, tag=tag)

    def sload(dst_ap, src_ap, skey, deps=(), tag="sload"):
        return P.add("sync", lambda g, d=dst_ap, s=src_ap: g.dma_start(out=d, in_=s),
                     deps=deps, dma=True, skey=skey, tag=tag)

    ld_consts = []
    ld_consts.append(gload(ident[:], d_ident[:], "c0"))
    ld_consts.append(gload(perm[:], d_perm[:], "c0"))
    ld_consts.append(gload(sinksT[:], d_sinksT[:], "c0"))
    ld_consts.append(gload(sinksN[:], d_sinksN[:], "c0"))
    ld_consts.append(gload(borow[:], d_borow[:], "c0"))
    ld_consts.append(gload(bvrow[:], d_bvrow[:], "c0"))
    ld_consts.append(gload(gateT[:], d_gate[:], "c0"))
    op_ones = P.add("gpsimd", lambda g: g.memset(ones[:], 1.0), tag="ones")
    op_mset = P.add("gpsimd", lambda g: g.memset(m_sb[:], 1.0), tag="m1")

    ld_ufm = []
    for i in range(4):
        src = d_ufm[:, i * 4 * TH:(i + 1) * 4 * TH]
        ld_ufm.append(gload(u_fm[:, i * 4 * TH:(i + 1) * 4 * TH], src, "c0",
                            tag="u_fm"))

    ld_masks = sload(masks[:], d_masks[:], "c1")
    ld_bq = sload(bq[:], d_bq[:], "c1")
    ld_bk = sload(bk[:], d_bk[:], "c1")
    ld_cs = [sload(Cq_sb[:], d_Cq[:], "c1"), sload(Sq_sb[:], d_Sq[:], "c1"),
             sload(Ck_sb[:], d_Ck[:], "c1"), sload(Sk_sb[:], d_Sk[:], "c1")]
    ld_constv = sload(constv[:], d_constv[:], "c1")

    v_evac = {}
    vb_prev = {}
    wv_ld = []
    for dvq in range(4):
        src = d_wv[:, dvq * NDC * 512:(dvq + 1) * NDC * 512]
        wv_ld.append(gload(wv_ring[dvq % 2][:], src, f"wv{dvq % 2}",
                           tag=f"wv{dvq}"))
    vmm = {}
    for dvq in range(4):
        for tt in range(NVT):
            bank = pf[(dvq * NVT + tt) % 2]
            def mk_vmm(dvq=dvq, tt=tt, bank=bank):
                def f(t):
                    last = None
                    for dc in range(NDC):
                        t.ldweights(u_fm[:, dc * TH + tt * 128: dc * TH + tt * 128 + 128])
                        last = t.matmul(bank[:, 0:512],
                                        u_fm[:, dc * TH + tt * 128: dc * TH + tt * 128 + 128],
                                        wv_ring[dvq % 2][:, dc * 512:(dc + 1) * 512],
                                        start=(dc == 0), stop=False)
                    t.ldweights(ones[0:1, :])
                    last = t.matmul(bank[:, 0:512], ones[0:1, :],
                                    bvrow[0:1, 2 * 512 * 0 + dvq * 512:(dvq + 1) * 512],
                                    start=False, stop=True)
                    return last
                return f
            deps = [wv_ld[dvq], ld_ufm[-1], op_ones,
                    ld_consts[5] if (dvq == 0 and tt == 0) else None,
                    v_evac.get((dvq * NVT + tt) - 2)]
            vmm[(dvq, tt)] = P.add("tensor", mk_vmm(), deps=deps, tag=f"vmm{dvq},{tt}")
            ev = P.add("scalar",
                       lambda s, dvq=dvq, tt=tt, bank=bank: s.copy(
                           v_tm[:, tt * D + dvq * 512: tt * D + (dvq + 1) * 512],
                           bank[:, 0:512]),
                       deps=[vmm[(dvq, tt)]], tag=f"vev{dvq},{tt}")
            v_evac[dvq * NVT + tt] = ev
        vb_prev[dvq] = vmm[(dvq, NVT - 1)]
        if dvq + 2 < 4:
            P.add_dep(wv_ld[dvq + 2], vb_prev[dvq])
    last_vev = v_evac[4 * NVT - 1]

    qk_mm = {}
    qk_ev = {}
    qk_perm = {}
    qk_t1 = {}
    qk_t2 = {}
    qk_add = {}
    wqk_ld = {}
    nreg = {}
    ridx = 0
    for og in range(32):
        isq = og < 16
        oc = og if isq else og - 16
        src = (d_wq if isq else d_wk)[:, oc * NDC * 128:(oc + 1) * NDC * 128]
        wqk_ld[og] = gload(wq_ring[og % 2][:], src, f"wq{og % 2}",
                           deps=[qk_mm.get(og - 2), wv_ld[-1] if og < 2 else None],
                           tag=f"wqk{og}")
        tss = [(64, 512), (576, 512)] if isq else [(0, 512), (512, 512), (1024, 128)]
        def mk_qkmm(og=og, tss=tss):
            def f(t):
                last = None
                for dc in range(NDC):
                    w = wq_ring[og % 2][:, dc * 128:(dc + 1) * 128]
                    t.ldweights(w)
                    for j, (t0, tn) in enumerate(tss):
                        last = t.matmul(pf[j][:, 0:tn], w,
                                        u_fm[:, dc * TH + t0: dc * TH + t0 + tn],
                                        start=(dc == 0), stop=(dc == NDC - 1))
                return last
            return f
        deps = [wqk_ld[og], ld_ufm[-1], last_vev if og == 0 else None]
        if og >= 1:
            deps += [qk_ev[(og - 1, j)] for j in range(nreg[og - 1])]
        nreg[og] = len(tss)
        qk_mm[og] = P.add("tensor", mk_qkmm(), deps=deps, tag=f"qkmm{og}")

        for j, (t0, tn) in enumerate(tss):
            r = ridx % 2
            bvec = bq if isq else bk
            def mk_ev(og=og, j=j, tn=tn, r=r, oc=oc, bvec=bvec):
                def f(s):
                    return s.activation(qraw[r][:, 0:tn],
                                        pf[j][:, 0:tn], AF.Identity,
                                        bias=bvec[:, oc:oc + 1], scale=1.0)
                return f
            qk_ev[(og, j)] = P.add("scalar", mk_ev(),
                                   deps=[qk_mm[og], ld_bq, ld_bk,
                                         qk_perm.get(ridx - 2), qk_t1.get(ridx - 2)],
                                   tag=f"qkev{og},{j}")
            def mk_perm(r=r, tn=tn, rb=4 + ridx % 2):
                def f(t):
                    t.ldweights(perm[:])
                    return t.matmul(pf[rb][:, 0:tn], perm[:], qraw[r][:, 0:tn],
                                    start=True, stop=True)
                return f
            qk_perm[ridx] = P.add("tensor", mk_perm(),
                                  deps=[qk_ev[(og, j)], ld_consts[1],
                                        qk_t2.get(ridx - 2)], tag=f"perm{ridx}")
            Ct = Cq_sb if isq else Ck_sb
            St = Sq_sb if isq else Sk_sb
            def mk_t1(r=r, t0=t0, tn=tn, Ct=Ct, isq=isq):
                c0 = (t0 - 64) if isq else t0
                def f(v):
                    return v.tensor_tensor(t1r[r][:, 0:tn], qraw[r][:, 0:tn],
                                           Ct[:, c0:c0 + tn], op=ALU.mult)
                return f
            qk_t1[ridx] = P.add("vector", mk_t1(),
                                deps=[qk_ev[(og, j)], ld_cs[0], ld_cs[2],
                                      qk_add.get(ridx - 2)], tag=f"t1_{ridx}")
            def mk_t2(r=r, t0=t0, tn=tn, St=St, isq=isq, rb=4 + ridx % 2):
                c0 = (t0 - 64) if isq else t0
                def f(v):
                    return v.tensor_tensor(t2r[r][:, 0:tn], pf[rb][:, 0:tn],
                                           St[:, c0:c0 + tn], op=ALU.mult)
                return f
            qk_t2[ridx] = P.add("vector", mk_t2(),
                                deps=[qk_perm[ridx], ld_cs[1], ld_cs[3],
                                      qk_add.get(ridx - 2)], tag=f"t2_{ridx}")
            dst = q_fm if isq else k_fm
            dcol = (oc * TOK + (t0 - 64)) if isq else (oc * TH + t0)
            def mk_add(r=r, tn=tn, dst=dst, dcol=dcol):
                def f(v):
                    return v.tensor_tensor(dst[:, dcol:dcol + tn], t1r[r][:, 0:tn],
                                           t2r[r][:, 0:tn], op=ALU.add)
                return f
            qk_add[ridx] = P.add("vector", mk_add(),
                                 deps=[qk_t1[ridx], qk_t2[ridx]], tag=f"radd{ridx}")
            ridx += 1
    n_regions = ridx
    last_q_add = qk_add[n_regions - 1]

    NIT = NH * NQT
    a_S = {}
    a_mask = {}
    a_max = {}
    a_exp = {}
    a_recip = {}
    a_pn = {}
    a_T = {}
    a_pTev = {}
    a_O = {}
    a_Oev = {}

    def it_hq(i):
        p, par = i // 2, i % 2
        return 2 * (p % (NH // 2)) + par, p // (NH // 2)

    def emit_S(i):
        h, qt = it_hq(i)
        ch, hp = h // 2, (h % 2) * 64
        bank = pf[i % 4]
        def f(t):
            qs = q_fm[hp:hp + 64, ch * TOK + qt * 128: ch * TOK + qt * 128 + 128]
            t.ldweights(qs, tile_position=(hp, 0))
            t.matmul(bank[0:128, 0:16], qs,
                     sinksT[hp:hp + 64, h * NSINK:(h + 1) * NSINK],
                     start=True, stop=False, tile_position=(hp, 0))
            return t.matmul(bank[0:128, 16:272], qs,
                            k_fm[hp:hp + 64, ch * TH + qt * 128: ch * TH + qt * 128 + 256],
                            start=False, stop=True, tile_position=(hp, 0))
        return f

    def emit_mask(i):
        h, qt = it_hq(i)
        typ = 0 if qt == 0 else (2 if qt == NQT - 1 else 1)
        def f(v):
            return v.tensor_tensor(S_sb[i % 4][:], pf[i % 4][0:128, 0:272],
                                   masks[:, typ * 272:(typ + 1) * 272], op=ALU.add)
        return f

    def emit_exp(i):
        def f(s):
            return s.activation(p_sb[i % 4][:], S_sb[i % 4][:], AF.Exp,
                                bias=0.0, scale=1.0,
                                accum_out=rowsum[:, i % 4:i % 4 + 1])
        return f

    def emit_recip(i):
        def f(v):
            return v.reciprocal(recipv[:, i % 4:i % 4 + 1], rowsum[:, i % 4:i % 4 + 1])
        return f

    def emit_pn(i):
        def f(s):
            return s.activation(pn_sb[i % 4][:], p_sb[i % 4][:], AF.Identity,
                                bias=0.0, scale=recipv[:, i % 4:i % 4 + 1])
        return f

    def emit_T(i):
        c0 = ((i // 2) % 2) * 512
        def f(t):
            t.transpose(pb[i % 2][0:128, c0:c0 + 128], pn_sb[i % 4][:, 16:144], ident[:])
            t.transpose(pb[i % 2][0:128, c0 + 128:c0 + 256], pn_sb[i % 4][:, 144:272], ident[:])
            return t.transpose(pb[i % 2][0:16, c0 + 256:c0 + 384], pn_sb[i % 4][:, 0:16], ident[:])
        return f

    def emit_pTev(i):
        c0 = ((i // 2) % 2) * 512
        def f(v):
            v.tensor_copy(pT_sb[i % 4][:, 0:256], pb[i % 2][0:128, c0:c0 + 256])
            return v.tensor_copy(pT_sb[i % 4][0:16, 256:384], pb[i % 2][0:16, c0 + 256:c0 + 384])
        return f

    def emit_O(i):
        h, qt = it_hq(i)
        bank = pf[4 + i % 2]
        def f(t):
            v0 = v_tm[:, qt * D + h * DH: qt * D + (h + 1) * DH]
            t.ldweights(v0)
            t.matmul(bank[0:64, 0:128], v0, pT_sb[i % 4][:, 0:128],
                     start=True, stop=False)
            v1 = v_tm[:, (qt + 1) * D + h * DH: (qt + 1) * D + (h + 1) * DH]
            t.ldweights(v1)
            t.matmul(bank[0:64, 0:128], v1, pT_sb[i % 4][:, 128:256],
                     start=False, stop=False)
            sn = sinksN[0:NSINK, h * DH:(h + 1) * DH]
            t.ldweights(sn)
            return t.matmul(bank[0:64, 0:128], sn, pT_sb[i % 4][0:16, 256:384],
                            start=False, stop=True)
        return f

    def emit_Oev(i):
        h, qt = it_hq(i)
        ch, hp = h // 2, (h % 2) * 64
        def f(s):
            return s.copy(y_fm[hp:hp + 64, ch * TOK + qt * 128: ch * TOK + qt * 128 + 128],
                          pf[4 + i % 2][0:64, 0:128])
        return f

    for w in range(0, NIT + 6, 2):
        for i in (w, w + 1):
            if i >= NIT:
                continue
            a_S[i] = P.add("tensor", emit_S(i),
                           deps=[a_mask.get(i - 4),
                                 last_q_add if i == 0 else None,
                                 ld_consts[2] if i == 0 else None],
                           tag=f"S{i}")
            a_mask[i] = P.add("vector", emit_mask(i),
                              deps=[a_S[i], a_exp.get(i - 4), ld_masks], tag=f"mk{i}")
            a_exp[i] = P.add("scalar", emit_exp(i),
                             deps=[a_mask[i], a_pn.get(i - 4), a_recip.get(i - 4)],
                             tag=f"ex{i}")
            a_recip[i] = P.add("vector", emit_recip(i),
                               deps=[a_exp[i], a_pn.get(i - 4)], tag=f"rc{i}")
            a_pn[i] = P.add("scalar", emit_pn(i),
                            deps=[a_recip[i], a_T.get(i - 4)], tag=f"pn{i}")
        for j in (w - 2, w - 1):
            if not (0 <= j < NIT):
                continue
            a_T[j] = P.add("tensor", emit_T(j),
                           deps=[a_pn[j], a_pTev.get(j - 2), ld_consts[0]], tag=f"T{j}")
            a_pTev[j] = P.add("vector", emit_pTev(j),
                              deps=[a_T[j], a_O.get(j - 4)], tag=f"pTe{j}")
        for k in (w - 4, w - 3):
            if not (0 <= k < NIT):
                continue
            a_O[k] = P.add("tensor", emit_O(k),
                           deps=[a_pTev[k], a_Oev.get(k - 2),
                                 ld_consts[3] if k == 0 else None,
                                 last_vev if k == 0 else None], tag=f"O{k}")
            a_Oev[k] = P.add("scalar", emit_Oev(k), deps=[a_O[k]], tag=f"Oe{k}")
    last_att_te = a_O[NIT - 1]
    last_att_ev = a_Oev[NIT - 1]

    wo_ld = {}
    wo_mm = {}
    ev_add = {}
    utm_ld = {}
    def wo_load(dq, deps):
        src = d_wo[:, dq * NDC * 512:(dq + 1) * NDC * 512]
        wo_ld[dq] = gload(wo_ring[dq % 2][:], src, f"wo{dq % 2}",
                          deps=deps, tag=f"woL{dq}")

    def utm_issue(dq):
        for tt in range(NTT):
            li = dq * NTT + tt
            utm_ld[(dq, tt)] = sload(
                utm_ring[li % 8][:],
                d_utm[tt * 128:(tt + 1) * 128, dq * 512:(dq + 1) * 512],
                f"utm{li % 8}",
                deps=[ev_add.get(li - 8) if li >= 8 else last_att_te],
                tag=f"utm{dq},{tt}")

    def wo_tile(dq, tt):
        li = dq * NTT + tt
        bank = pf[li % 2]
        def mk_womm(dq=dq, tt=tt, bank=bank):
            def f(t):
                for yc in range(NDC):
                    ys = y_fm[:, yc * TOK + tt * 128: yc * TOK + tt * 128 + 128]
                    t.ldweights(ys)
                    t.matmul(bank[:, 0:512], ys,
                             wo_ring[dq % 2][:, yc * 512:(yc + 1) * 512],
                             start=(yc == 0), stop=False)
                t.ldweights(ones[0:1, :])
                return t.matmul(bank[:, 0:512], ones[0:1, :],
                                borow[0:1, dq * 512:(dq + 1) * 512],
                                start=False, stop=True)
            return f
        wo_mm[(dq, tt)] = P.add("tensor", mk_womm(),
                                deps=[wo_ld[dq], last_att_ev if li == 0 else None,
                                      ld_consts[4], op_ones,
                                      ev_add.get(li - 2)], tag=f"wo{dq},{tt}")
        def mk_evadd(dq=dq, tt=tt, bank=bank, li=li):
            def f(v):
                return v.tensor_tensor(acc[:, tt * D + dq * 512: tt * D + (dq + 1) * 512],
                                       bank[:, 0:512], utm_ring[li % 8][:],
                                       op=ALU.add)
            return f
        ev_add[li] = P.add("vector", mk_evadd(),
                           deps=[wo_mm[(dq, tt)], utm_ld[(dq, tt)],
                                 last_att_te if li == 0 else None],
                           tag=f"eva{dq},{tt}")

    ln_done = {}
    x1eva = {}
    x1evb = {}
    x1Tb_d = {}
    gate_m = {}
    gmm = {}
    g1p = [None]

    def ln_chain(tt, dep0, accs, pre):
        cs = slice(tt, tt + 1)
        s1 = P.add("vector", lambda v, a=accs, cs=cs: v.reduce_sum(ln_s[:, cs], a, axis=AX.X),
                   deps=dep0, tag=f"{pre}s{tt}")
        s2 = P.add("scalar", lambda s, a=accs, cs=cs: s.activation(
                        sqs[:], a, AF.Square, accum_out=ln_ss[:, cs]),
                   deps=dep0 + [ln_chain.prev_s3], tag=f"{pre}sq{tt}")
        ln_chain.prev_s3 = s2
        s4 = P.add("scalar", lambda s, cs=cs: s.mul(ln_mean[:, cs], ln_s[:, cs], 1.0 / D),
                   deps=[s1, ld_constv], tag=f"{pre}mean{tt}")
        s5 = P.add("scalar", lambda s, cs=cs: s.mul(ln_msq[:, cs], ln_ss[:, cs], 1.0 / D),
                   deps=[s2], tag=f"{pre}msq{tt}")
        s7 = P.add("vector", lambda v, cs=cs: v.scalar_tensor_tensor(
                        ln_var[:, cs], ln_mean[:, cs], ln_mean[:, cs], ln_msq[:, cs],
                        op0=ALU.mult, op1=ALU.subtract),
                   deps=[s4, s5], tag=f"{pre}var{tt}")
        s8 = P.add("scalar", lambda s, cs=cs: s.activation(
                        ln_var[:, cs], ln_var[:, cs], AF.Identity,
                        bias=constv[:, 1:2], scale=-1.0),
                   deps=[s7], tag=f"{pre}vpe{tt}")
        s9 = P.add("vector", lambda v, cs=cs: v.reciprocal(ln_var[:, cs], ln_var[:, cs]),
                   deps=[s8], tag=f"{pre}rv{tt}")
        s10 = P.add("scalar", lambda s, cs=cs: s.activation(ln_rstd[:, cs], ln_var[:, cs],
                                                            AF.Sqrt),
                    deps=[s9], tag=f"{pre}rstd{tt}")
        s11 = P.add("vector", lambda v, cs=cs: v.scalar_tensor_tensor(
                        ln_nmrs[:, cs], ln_mean[:, cs], -1.0, ln_rstd[:, cs],
                        op0=ALU.mult, op1=ALU.mult),
                    deps=[s10, s4, ld_constv], tag=f"{pre}nmrs{tt}")
        return s11

    ln_chain.prev_s3 = None

    def ln1_tile(tt):
        dep0 = [ev_add[3 * NTT + tt]]
        accs = acc[:, tt * D:(tt + 1) * D]
        s11 = ln_chain(tt, dep0, accs, "ln_")
        s12 = P.add("scalar", lambda s, a=accs, tt=tt: s.activation(
                        a, a, AF.Identity, bias=ln_nmrs[:, tt:tt + 1],
                        scale=ln_rstd[:, tt:tt + 1]),
                    deps=[s11], tag=f"ln_app{tt}")
        ln_done[tt] = s12
        s13a = P.add("scalar", lambda s, a=accs, tt=tt: s.copy(
                         x1bf[tt % 2][:, 0:1024], a[:, 0:1024]),
                     deps=[s12, x1Tb_d.get(tt - 2)], tag=f"x1bfa{tt}")
        s13b = P.add("vector", lambda v, a=accs, tt=tt: v.tensor_copy(
                         x1bf[tt % 2][:, 1024:2048], a[:, 1024:2048]),
                     deps=[s12, x1Tb_d.get(tt - 2)], tag=f"x1bfb{tt}")
        def mk_x1T(tt, half):
            def f(t):
                last = None
                for dc in range(half * 8, half * 8 + 8):
                    last = t.transpose(pb[half][0:128, (dc % 8) * 128:(dc % 8 + 1) * 128],
                                       x1bf[tt % 2][:, dc * 128:(dc + 1) * 128], ident[:])
                return last
            return f
        x1Ta = P.add("tensor", mk_x1T(tt, 0),
                     deps=[s13a, x1eva.get(tt - 1), ld_consts[0],
                           a_pTev.get(NIT - 1) if tt == 0 else None], tag=f"x1Ta{tt}")
        x1Tb = P.add("tensor", mk_x1T(tt, 1),
                     deps=[s13b, x1evb.get(tt - 1),
                           a_pTev.get(NIT - 1) if tt == 0 else None], tag=f"x1Tb{tt}")
        x1Tb_d[tt] = x1Tb
        def mk_x1ev(tt, half):
            def f(v):
                return v.tensor_copy(
                    x1_fm[:].rearrange("p (c t) -> p c t", c=NDC)
                    [:, half * 8:half * 8 + 8, tt * 128:(tt + 1) * 128],
                    pb[half][0:128, 0:1024].rearrange("p (c t) -> p c t", c=8))
            return f
        x1eva[tt] = P.add("vector", mk_x1ev(tt, 0),
                          deps=[x1Ta, last_att_te if tt == 0 else None],
                          tag=f"x1eva{tt}")
        x1evb[tt] = P.add("vector", mk_x1ev(tt, 1), deps=[x1Tb], tag=f"x1evb{tt}")

    def gate_tile(tt):
        def mk_gmm(tt=tt):
            def f(t):
                last = None
                for dc in range(NDC):
                    xs = x1_fm[:, dc * TOK + tt * 128: dc * TOK + tt * 128 + 128]
                    t.ldweights(xs)
                    last = t.matmul(pf[4][:, 0:8], xs, gateT[:, dc * 8:(dc + 1) * 8],
                                    start=(dc == 0), stop=(dc == NDC - 1))
                return last
            return f
        gmm[tt] = P.add("tensor", mk_gmm(),
                        deps=[x1eva[tt], x1evb[tt], ld_consts[6], g1p[0]],
                        tag=f"gmm{tt}")
        g8 = slice(tt * 8, tt * 8 + 8)
        ct = slice(tt, tt + 1)
        g1 = P.add("scalar", lambda s, g8=g8: s.activation(sc_gate[:, g8], pf[4][:, 0:8],
                                                           AF.Sigmoid),
                   deps=[gmm[tt]], tag=f"gsig{tt}")
        g1p[0] = g1
        g2 = P.add("vector", lambda v, g8=g8, ct=ct: v.reduce_max(
                        g_max1[:, ct], sc_gate[:, g8], axis=AX.X),
                   deps=[g1], tag=f"gm1{tt}")
        g3 = P.add("vector", lambda v, g8=g8, ct=ct: v.tensor_scalar(
                        g_eq[:, g8], sc_gate[:, g8],
                        g_max1[:, ct], None, op0=ALU.is_equal),
                   deps=[g2], tag=f"geq{tt}")
        g4 = P.add("vector", lambda v, g8=g8: v.scalar_tensor_tensor(
                        g_msk[:, g8], g_eq[:, g8], NEG, sc_gate[:, g8],
                        op0=ALU.mult, op1=ALU.add),
                   deps=[g3, ld_constv], tag=f"gmsk{tt}")
        g5 = P.add("vector", lambda v, g8=g8, ct=ct: v.reduce_max(
                        g_max2[:, ct], g_msk[:, g8], axis=AX.X),
                   deps=[g4], tag=f"gm2{tt}")
        g6 = P.add("vector", lambda v, g8=g8, ct=ct: v.tensor_scalar(
                        g_ge[:, g8], sc_gate[:, g8],
                        g_max2[:, ct], None, op0=ALU.is_ge),
                   deps=[g5], tag=f"gge{tt}")
        g7 = P.add("vector", lambda v, tt=tt, g8=g8: v.tensor_tensor(
                        m_sb[:, tt * 9:tt * 9 + 8], sc_gate[:, g8], g_ge[:, g8],
                        op=ALU.mult),
                   deps=[g6, op_mset], tag=f"gm{tt}")
        gate_m[tt] = g7

    w13_ld = {}
    w2_ld = {}
    ab_mm = {}
    h_silu = {}
    h_stt = {}
    h_T = {}
    h_ev = {}
    w2_mm = {}
    acc_add = {}
    lastw2 = {}

    def abX_tile(hq, tt):
        slot = w13_ring[hq % 2]
        bank = pf[tt % 4]
        def mk_ab(slot=slot, tt=tt, bank=bank):
            def f(t):
                last = None
                for dc in range(NDC):
                    xs = x1_fm[:, dc * TOK + tt * 128: dc * TOK + tt * 128 + 128]
                    t.ldweights(xs)
                    last = t.matmul(bank[:, 0:512], xs,
                                    slot[:, dc * 512:(dc + 1) * 512],
                                    start=(dc == 0), stop=(dc == NDC - 1))
                return last
            return f
        prev = h_stt.get((0, 0, tt - 4))
        deps = [w13_ld[(0, hq, 1)],
                x1eva.get(tt), x1evb.get(tt), prev,
                ev_add.get(4 * NTT - 1) if tt < 4 else None]
        ab_mm[(hq, tt)] = P.add("tensor", mk_ab(), deps=deps, tag=f"ab0,{hq},{tt}")
        hcol = tt * HID + hq * 256
        def mk_silu(tt=tt, bank=bank, hcol=hcol):
            def f(s):
                return s.activation(h_tm[:, hcol:hcol + 256], bank[:, 0:256],
                                    AF.Silu, bias=0.0,
                                    scale=m_sb[:, tt * 9: tt * 9 + 1])
            return f
        h_silu[(0, hq, tt)] = P.add(
            "scalar", mk_silu(),
            deps=[ab_mm[(hq, tt)], gate_m[tt], wo_mm[(3, NTT - 1)]],
            tag=f"silu0,{hq},{tt}")
        def mk_stt(tt=tt, bank=bank, hcol=hcol):
            def f(v):
                return v.scalar_tensor_tensor(
                    h_tm[:, hcol:hcol + 256], bank[:, 256:512],
                    m_sb[:, tt * 9: tt * 9 + 1],
                    h_tm[:, hcol:hcol + 256], op0=ALU.mult, op1=ALU.mult)
            return f
        h_stt[(0, hq, tt)] = P.add("vector", mk_stt(),
                                   deps=[h_silu[(0, hq, tt)]],
                                   tag=f"hstt0,{hq},{tt}")

    wo_load(0, [a_O[4 * NH - 1]])
    wo_load(1, [last_att_te])
    utm_issue(0)
    for tt in range(NTT):
        wo_tile(0, tt)
    utm_issue(1)
    wo_load(2, [wo_mm[(0, NTT - 1)]])
    for tt in range(NTT):
        wo_tile(1, tt)
    utm_issue(2)
    wo_load(3, [wo_mm[(1, NTT - 1)]])
    for tt in range(NTT):
        wo_tile(2, tt)
    utm_issue(3)
    for tt in range(NTT):
        wo_tile(3, tt)
    for hq in (0,):
        op = gload(w13_ring[hq][:],
                   d_w13[0][:, hq * NDC * 512:(hq + 1) * NDC * 512], f"w13{hq}",
                   deps=[wo_mm[(3, NTT - 1)]], tag=f"w13L0,{hq}")
        w13_ld[(0, hq, 1)] = w13_ld[(0, hq, 3)] = op
    ln1_tile(0)
    ln1_tile(1)
    for tt in range(NTT):
        if tt + 2 < NTT:
            ln1_tile(tt + 2)
        gate_tile(tt)
        abX_tile(0, tt)

    for e in range(NE + 1):
        s13d = d_w13[e]
        s2d = d_w2[e]
        for hq in range(4):
            li = e * 4 + hq
            if (e, hq, 1) in w13_ld:
                continue
            slot = w13_ring[li % 2]
            dep = [ab_mm.get((li - 2, NTT - 1)),
                   wo_mm.get((3, NTT - 1)) if li < 2 else None]
            op = gload(slot[:],
                       s13d[:, hq * NDC * 512:(hq + 1) * NDC * 512],
                       f"w13{li % 2}",
                       deps=dep, tag=f"w13L{e},{hq}")
            w13_ld[(e, hq, 1)] = w13_ld[(e, hq, 3)] = op
        for dq in range(4):
            li = e * 4 + dq
            slot = w2_ring[li % 2]
            dep = [w2_mm.get(((li - 2) // 4, (li - 2) % 4, NTT - 1))
                   if li >= 2 and (li - 2) // 4 < e else None,
                   ev_add.get(4 * NTT - 1) if li < 2 else None]
            w2_ld[(e, dq)] = gload(
                slot[:],
                s2d[:, dq * NHC * 512:(dq + 1) * NHC * 512],
                f"w2{li % 2}",
                deps=dep, tag=f"w2L{e},{dq}")

        for hq in range(4):
            li = e * 4 + hq
            slot = w13_ring[li % 2]
            for tt in range(NTT):
                bank = pf[tt % 4]
                def mk_ab(slot=slot, tt=tt, bank=bank):
                    def f(t):
                        last = None
                        for dc in range(NDC):
                            xs = x1_fm[:, dc * TOK + tt * 128: dc * TOK + tt * 128 + 128]
                            t.ldweights(xs)
                            last = t.matmul(bank[:, 0:512], xs,
                                            slot[:, dc * 512:(dc + 1) * 512],
                                            start=(dc == 0), stop=(dc == NDC - 1))
                        return last
                    return f
                if (li, tt) not in ab_mm:
                    prev_stt = (h_stt.get((e, hq, tt - 4)) if tt >= 4 else
                                h_stt.get((e, hq - 1, NTT - 4 + tt)) if hq > 0 else
                                h_stt.get((e - 1, 3, NTT - 4 + tt)))
                    deps = [w13_ld[(e, hq, 1)], prev_stt]
                    ab_mm[(li, tt)] = P.add("tensor", mk_ab(), deps=deps,
                                            tag=f"ab{e},{hq},{tt}")
                if (e, hq, tt) in h_silu:
                    continue
                hcol = tt * HID + hq * 256
                def mk_silu(e=e, tt=tt, bank=bank, hcol=hcol):
                    def f(s):
                        return s.activation(h_tm[:, hcol:hcol + 256], bank[:, 0:256],
                                            AF.Silu, bias=0.0,
                                            scale=m_sb[:, tt * 9 + e: tt * 9 + e + 1])
                    return f
                deps = [ab_mm[(li, tt)], gate_m[tt] if e < NE else op_mset,
                        h_T.get((e - 1, tt)),
                        wo_mm.get((3, NTT - 1)) if e == 0 else None,
                        w2_mm.get((e - 1 if e > 0 else 0, NTT - 1)) if e == 1 else None]
                h_silu[(e, hq, tt)] = P.add("scalar", mk_silu(), deps=deps,
                                            tag=f"silu{e},{hq},{tt}")
                def mk_stt(e=e, tt=tt, bank=bank, hcol=hcol):
                    def f(v):
                        return v.scalar_tensor_tensor(
                            h_tm[:, hcol:hcol + 256], bank[:, 256:512],
                            m_sb[:, tt * 9 + e: tt * 9 + e + 1],
                            h_tm[:, hcol:hcol + 256], op0=ALU.mult, op1=ALU.mult)
                    return f
                h_stt[(e, hq, tt)] = P.add("vector", mk_stt(),
                                           deps=[h_silu[(e, hq, tt)]],
                                           tag=f"hstt{e},{hq},{tt}")
        for hq in (2, 3):
            gate = ab_mm[(e * 4 + hq - 2, NTT - 1)]
            P.add_dep(w13_ld[(e, hq, 1)], gate)
            P.add_dep(w13_ld[(e, hq, 3)], gate)
        for tt in range(NTT):
            def mk_hT(e=e, tt=tt):
                def f(t):
                    last = None
                    for hc in range(8):
                        last = t.transpose(pb[tt % 2][0:128, hc * 128:(hc + 1) * 128],
                                           h_tm[:, tt * HID + hc * 128: tt * HID + (hc + 1) * 128],
                                           ident[:])
                    return last
                return f
            deps = [h_stt[(e, hq2, tt)] for hq2 in range(4)]
            deps += [h_ev.get((e, tt - 2)) if tt >= 2
                     else h_ev.get((e - 1, NTT - 2 + tt)), ld_consts[0]]
            if e == 0 and tt < 2:
                deps += [x1eva.get(NTT - 1), x1evb.get(NTT - 1)]
            h_T[(e, tt)] = P.add("tensor", mk_hT(), deps=deps, tag=f"hT{e},{tt}")
            def mk_hev(e=e, tt=tt):
                def f(v):
                    return v.tensor_copy(
                        h_fm[:].rearrange("p (c t) -> p c t", c=NHC)[:, :, tt * 128:(tt + 1) * 128],
                        pb[tt % 2][0:128, 0:1024].rearrange("p (c t) -> p c t", c=8))
                return f
            h_ev[(e, tt)] = P.add("vector", mk_hev(),
                                  deps=[h_T[(e, tt)],
                                        w2_mm.get((e - 1, tt)),
                                        wo_mm.get((3, NTT - 1)) if e == 0 else None],
                                  tag=f"hev{e},{tt}")
        for dq in range(4):
            li2 = e * 4 + dq
            for tt in range(NTT):
                bank = pf[4 + (dq * NTT + tt) % 2]
                def mk_w2(e=e, dq=dq, tt=tt, bank=bank, li2=li2):
                    def f(t):
                        last = None
                        for hc in range(NHC):
                            hs = h_fm[:, hc * TOK + tt * 128: hc * TOK + tt * 128 + 128]
                            t.ldweights(hs)
                            last = t.matmul(bank[:, 0:512], hs,
                                            w2_ring[li2 % 2][:, hc * 512:(hc + 1) * 512],
                                            start=(hc == 0), stop=(hc == NHC - 1))
                        return last
                    return f
                deps = [w2_ld[(e, dq)], h_ev[(e, tt)],
                        acc_add.get((e, dq, tt - 2) if tt >= 2 else
                                    (e, dq - 1, NTT - 2 + tt) if dq > 0 else
                                    (e - 1, 3, NTT - 2 + tt))]
                op = P.add("tensor", mk_w2(), deps=deps, tag=f"w2{e},{dq},{tt}")
                if dq == 3:
                    w2_mm[(e, tt)] = op
                w2_mm[(e, dq, tt)] = op
                def mk_acc(e=e, dq=dq, tt=tt, bank=bank):
                    def f(v):
                        a = acc[:, tt * D + dq * 512: tt * D + (dq + 1) * 512]
                        return v.tensor_tensor(a, bank[:, 0:512], a, op=ALU.add)
                    return f
                acc_add[(e, dq, tt)] = P.add(
                    "vector", mk_acc(),
                    deps=[w2_mm[(e, dq, tt)],
                          ln_done[tt] if e == 0 else None], tag=f"acc{e},{dq},{tt}")
        for dq in (2, 3):
            P.add_dep(w2_ld[(e, dq)], w2_mm[(e, dq - 2, NTT - 1)])
        lastw2[e] = w2_mm[(e, 3, NTT - 1)]

    st_ops = {}
    for tt in range(NTT):
        accs = acc[:, tt * D:(tt + 1) * D]
        dep0 = [acc_add[(NE, 3, tt)]] + [acc_add[(NE, dqq, tt)] for dqq in range(4)]
        s11 = ln_chain(tt, dep0, accs, "l2_")
        s12 = P.add("scalar", lambda s, a=accs, tt=tt: s.activation(
                        out_ring[tt % 2][:], a, AF.Identity, bias=ln_nmrs[:, tt:tt + 1],
                        scale=ln_rstd[:, tt:tt + 1]),
                    deps=[s11, st_ops.get(tt - 2),
                          lastw2[NE] if tt == 0 else None], tag=f"l2app{tt}")
        def mk_st(tt=tt):
            def f(s):
                return s.dma_start(out=d_out[tt * 128:(tt + 1) * 128, :],
                                   in_=out_ring[tt % 2][:])
            return f
        st_ops[tt] = P.add("sync", mk_st(), deps=[s12], dma=True,
                           skey=f"out{tt % 2}", tag=f"st{tt}")

    P.finalize(nc)

    with nc.Block() as block:
        @block.tensor
        def _(t):
            P.emit_engine("tensor", t)

        @block.vector
        def _(v):
            P.emit_engine("vector", v)

        @block.scalar
        def _(s):
            P.emit_engine("scalar", s)

        @block.gpsimd
        def _(g):
            P.emit_engine("gpsimd", g)

        @block.sync
        def _(s):
            P.emit_engine("sync", s)
            s.wait_ge(P.sems["out0"], P.final_counts["out0"])
            s.wait_ge(P.sems["out1"], P.final_counts["out1"])
    return nc


def _rope_tables(FT):
    import jax
    import jax.numpy as jnp
    cpu = jax.devices("cpu")[0]
    with jax.default_device(cpu):
        half = 32
        inv_freq = 1.0 / 10000.0 ** (jnp.arange(half, dtype=jnp.float32) / half)
        ang = jnp.arange(8192, dtype=jnp.float32)[:, None] * inv_freq
        cos = np.asarray(jnp.cos(ang), dtype=np.float32).astype(FT)
        sin = np.asarray(jnp.sin(ang), dtype=np.float32).astype(FT)
    return cos, sin


def _band_masks(core, ncores, nqt, FT):
    m = np.zeros((3, 128, 272), dtype=FT)
    r = np.arange(128)[:, None]
    j = np.arange(256)[None, :]
    band = np.where((j >= r) & (j <= r + 128), 0.0, NEG).astype(FT)
    for t in range(3):
        m[t, :, 16:] = band
    if core == 0:
        m[0, :, 16:16 + 64] = NEG
    if core == ncores - 1:
        m[2, :, 16 + 192:] = NEG
    return m


def _fm_chunks(WT, cpc):
    K, N = WT.shape
    c = K // 128
    nq = N // cpc
    a = WT.reshape(c, 128, nq, cpc).transpose(1, 2, 0, 3)
    return np.ascontiguousarray(a.reshape(128, nq * c * cpc))


def prep_inmaps(L0_attn_u, L0_attn_Wqkv, L0_attn_bqkv, L0_attn_Wo, L0_attn_bo,
                L0_attn_sinks, L0_ffn_gate_w, L0_ffn_w1, L0_ffn_w2, L0_ffn_w3,
                L0_ffn_ws1, L0_ffn_ws2, L0_ffn_ws3):
    FT = np.float64 if np.asarray(L0_attn_u).dtype == np.float64 else np.float32
    u = np.ascontiguousarray(np.asarray(L0_attn_u, dtype=FT)[0])
    L = u.shape[0]
    TOK = L // NCORES
    TH = TOK + 128
    NQT = TOK // 128

    wqkvT = np.ascontiguousarray(np.asarray(L0_attn_Wqkv, FT).T)
    wq_h = _fm_chunks(wqkvT[:, 0:D], 128)
    wk_h = _fm_chunks(wqkvT[:, D:2 * D], 128)
    wv_h = _fm_chunks(wqkvT[:, 2 * D:3 * D], 512)
    bqkv = np.asarray(L0_attn_bqkv, FT)
    bq = np.ascontiguousarray(bqkv[0:D].reshape(NDC, 128).T)
    bk = np.ascontiguousarray(bqkv[D:2 * D].reshape(NDC, 128).T)
    bv_row = np.ascontiguousarray(bqkv[2 * D:3 * D].reshape(1, D))
    wo_h = _fm_chunks(np.asarray(L0_attn_Wo, FT).T, 512)
    bo_row = np.ascontiguousarray(np.asarray(L0_attn_bo, FT).reshape(1, D))
    sinks = np.asarray(L0_attn_sinks, FT)
    sinksN = np.ascontiguousarray(sinks.transpose(1, 0, 2).reshape(NSINK, NH * DH))
    sT = sinks.transpose(0, 2, 1).reshape(NH, DH, NSINK)
    sinksT = np.zeros((128, NH * NSINK), dtype=FT)
    for h in range(NH):
        sinksT[0:64, h * NSINK:(h + 1) * NSINK] = sT[h]
        sinksT[64:128, h * NSINK:(h + 1) * NSINK] = sT[h]
    gate_h = _fm_chunks(np.asarray(L0_ffn_gate_w, FT).T, NE)
    w1T = np.asarray(L0_ffn_w1, FT).transpose(0, 2, 1)
    w3T = np.asarray(L0_ffn_w3, FT).transpose(0, 2, 1)
    w2T = np.asarray(L0_ffn_w2, FT).transpose(0, 2, 1)

    def _w13_block(w1m, w3m):
        blocks = []
        for hq in range(4):
            cat = np.concatenate([w1m[:, hq * 256:(hq + 1) * 256],
                                  w3m[:, hq * 256:(hq + 1) * 256]], axis=1)
            blocks.append(_fm_chunks(cat, 512))
        return np.concatenate(blocks, axis=1)

    w13_h = np.stack([_w13_block(w1T[e], w3T[e]) for e in range(NE)]
                     + [_w13_block(np.asarray(L0_ffn_ws1, FT).T,
                                   np.asarray(L0_ffn_ws3, FT).T)])
    w2_h = np.stack([_fm_chunks(w2T[e], 512) for e in range(NE)]
                    + [_fm_chunks(np.asarray(L0_ffn_ws2, FT).T, 512)])

    constv = np.tile(np.array([1.0 / D, 1e-5, -1.0, NEG], dtype=FT), (128, 1))
    ident = np.eye(128, dtype=FT)
    perm = np.zeros((128, 128), dtype=FT)
    for b in range(4):
        i = np.arange(32)
        perm[b * 32 + i, (b ^ 1) * 32 + i] = 1.0
    cos, sin = _rope_tables(FT)

    in_maps = []
    for c in range(NCORES):
        lo, hi = c * TOK - 64, c * TOK + TOK + 64
        upad = np.zeros((TH, D), dtype=FT)
        s0, s1 = max(lo, 0), min(hi, L)
        upad[s0 - lo: s1 - lo] = u[s0:s1]
        u_fm_h = _fm_chunks(np.ascontiguousarray(upad.T), TH)
        pos = np.clip(np.arange(lo, hi), 0, L - 1)
        ck = cos[pos].T
        sk = sin[pos].T
        Ck = np.ascontiguousarray(np.tile(ck, (4, 1)))
        Sk = np.ascontiguousarray(np.concatenate([-sk, sk, -sk, sk], axis=0))
        Cq = np.ascontiguousarray(Ck[:, 64:64 + TOK] * FT(0.125))
        Sq = np.ascontiguousarray(Sk[:, 64:64 + TOK] * FT(0.125))
        masks_h = np.ascontiguousarray(
            _band_masks(c, NCORES, NQT, FT).transpose(1, 0, 2).reshape(128, 3 * 272))
        in_maps.append({
            "u_fm_d": u_fm_h,
            "u_tm": np.ascontiguousarray(u[c * TOK:(c + 1) * TOK]),
            "wq_d": wq_h, "wk_d": wk_h, "wv_d": wv_h,
            "bq": bq, "bk": bk, "bv_row": bv_row,
            "wo_d": wo_h, "bo_row": bo_row, "sinksN": sinksN, "sinksT": sinksT,
            "Cq": Cq, "Sq": Sq, "Ck": Ck, "Sk": Sk,
            "masks": masks_h,
            "ident": ident, "perm": perm, "gate_d": gate_h,
            "w13_d": w13_h, "w2_d": w2_h,
            "constv": constv,
        })
    return in_maps, TOK, L, FT


def kernel(**inputs):
    in_maps, TOK, L, FT = prep_inmaps(**inputs)
    nc = build_program(TOK)
    res = run_bass_kernel_spmd(nc, in_maps, core_ids=list(range(NCORES)))
    out = np.concatenate([np.asarray(r["y_out"]) for r in res.results], axis=0)
    return out.reshape(1, L, D).astype(FT)
